# version 03
# baseline (speedup 1.0000x reference)
import contextlib
import numpy as np
import ml_dtypes
import concourse.bass as bass
import concourse.mybir as mybir
from concourse.bass_utils import run_bass_kernel_spmd

F32 = mybir.dt.float32
BF16 = mybir.dt.bfloat16
I32 = mybir.dt.int32
AF = mybir.ActivationFunctionType
ALU = mybir.AluOpType
AX = mybir.AxisListType

T = 2048
D = 1024
DC = 8
NT = 16
H0, HD0 = 16, 64
FF = 4096
WIN = 144
NH, DK, DV, KD, VD = 4, 128, 256, 512, 1024
RLO, R5, GD = 32, 160, 64
NE, NI = 8, 2048
EPS_LN = 1e-5
EPS_GN = 1e-5
JIT2 = 0.02
NEG = -1e30

ENGK = ("t", "v", "s", "g", "q")
ENGNAME = dict(t="tensor", v="vector", s="scalar", g="gpsimd", q="sync")


class K:

    def __init__(self, nc, es):
        self.nc = nc
        self.es = es
        self.e = {k: getattr(nc, ENGNAME[k]) for k in ENGK}
        self.sem = {k: es.enter_context(nc.semaphore("S_" + k)) for k in ENGK}
        self.dsem = {k: es.enter_context(nc.semaphore("D_" + k)) for k in ("q", "s", "g")}
        self.cnt = {k: 0 for k in ENGK}
        self.dcnt = {k: 0 for k in ("q", "s", "g")}
        self.waited = {}
        self.last = {k: None for k in ENGK}
        self.bank_guard = {}

    def op(self, ek, name, *a, **kw):
        instr = getattr(self.e[ek], name)(*a, **kw)
        self.last[ek] = instr
        return instr

    def inc(self, ek):
        if self.last[ek] is not None:
            self.cnt[ek] += 1
            self.last[ek].then_inc(self.sem[ek], 1)
            self.last[ek] = None
        return ("c", ek, self.cnt[ek])

    def tok(self, ek):
        return ("c", ek, self.cnt[ek])

    def dma(self, qk, out, in_, wait=(), **kw):
        for tk in wait:
            self.wait(qk, tk)
        i = self.e[qk].dma_start(out=out, in_=in_, **kw)
        self.dcnt[qk] += 16
        i.then_inc(self.dsem[qk], 16)
        return ("d", qk, self.dcnt[qk])

    def wait(self, ek, *toks):
        for tk in toks:
            if tk is None:
                continue
            kind, src, val = tk
            if val <= 0:
                continue
            key = (ek, kind, src)
            if self.waited.get(key, 0) >= val:
                continue
            sem = self.sem[src] if kind == "c" else self.dsem[src]
            self.e[ek].wait_ge(sem, val)
            self.waited[key] = val

    def fence_all(self, waiters=ENGK):
        toks = [self.inc(src) for src in ENGK]
        toks += [("d", qk, self.dcnt[qk]) for qk in ("q", "s", "g")]
        for ek in waiters:
            self.wait(ek, *toks)


class Consts:
    pass


def alloc_consts(k):
    nc, es = k.nc, k.es
    S = lambda *a, **kw: es.enter_context(nc.sbuf_tensor(*a, **kw))
    C = Consts()
    C.iota = S("c_iota", [128, 256], I32)
    C.ident_f = S("c_ident_f", [128, 128], F32)
    C.ident_b = S("c_ident_b", [128, 128], BF16)
    C.band_b = S("c_band_b", [128, WIN], BF16)
    C.supper_b = S("c_supper_b", [128, 128], BF16)
    C.supper4 = S("c_supper4", [32, 128], BF16)
    C.ident4 = S("c_ident4", [32, 128], BF16)
    C.ones_col_f = S("c_ones_col_f", [128, 1], F32)
    C.ones_col_b = S("c_ones_col_b", [128, 1], BF16)
    C.ones_row_f = S("c_ones_row_f", [1, 128], F32)
    C.ones_row_b = S("c_ones_row_b", [1, 128], BF16)
    C.scr_f = S("c_scr_f", [128, 2 * WIN], F32)
    C.eps_ln = S("c_eps_ln", [128, 1], F32)
    C.m12 = S("c_m12", [128, 1], F32)
    C.zero_f = S("c_zero_f", [128, 1], F32)
    k.C = C
    g, v = "g", "v"
    k.op(g, "iota", out=C.iota[:], pattern=[[1, 256]], base=0, channel_multiplier=-1)
    ti = k.inc(g)
    k.wait(v, ti)
    k.op(v, "tensor_single_scalar", out=C.ident_f[:], in_=C.iota[:, 0:128], scalar=0, op=ALU.is_equal)
    k.op(v, "tensor_single_scalar", out=C.ident_b[:], in_=C.iota[:, 0:128], scalar=0, op=ALU.is_equal)
    for b in range(4):
        k.op(v, "tensor_single_scalar", out=C.supper4[0:32, b * 32:(b + 1) * 32],
             in_=C.iota[0:32, b * 32:(b + 1) * 32], scalar=b * 32, op=ALU.is_gt)
        k.op(v, "tensor_single_scalar", out=C.ident4[0:32, b * 32:(b + 1) * 32],
             in_=C.iota[0:32, b * 32:(b + 1) * 32], scalar=b * 32, op=ALU.is_equal)
    k.op(v, "tensor_single_scalar", out=C.scr_f[:, 0:WIN], in_=C.iota[:, 0:WIN], scalar=0, op=ALU.is_ge)
    k.op(v, "tensor_single_scalar", out=C.scr_f[:, WIN:2 * WIN], in_=C.iota[:, 0:WIN], scalar=16, op=ALU.is_le)
    tv = k.inc(v)
    k.wait(v, tv)
    k.op(v, "tensor_tensor", out=C.band_b[:], in0=C.scr_f[:, 0:WIN], in1=C.scr_f[:, WIN:2 * WIN], op=ALU.mult)
    k.op(v, "tensor_single_scalar", out=C.supper_b[:], in_=C.iota[:, 0:128], scalar=0, op=ALU.is_gt)
    k.op(v, "memset", C.ones_col_f[:], 1.0)
    k.op(v, "memset", C.ones_col_b[:], 1.0)
    k.op(v, "memset", C.ones_row_f[:], 1.0)
    k.op(v, "memset", C.ones_row_b[:], 1.0)
    k.op(v, "memset", C.eps_ln[:], EPS_LN)
    k.op(v, "memset", C.m12[:], -12.0)
    k.op(v, "memset", C.zero_f[:], 0.0)
    k.inc(v)
    k.fence_all()


def emit_bcast_row(k, dst, src_row, psum, n):
    C = k.C
    ts = None
    for c0 in range(0, n, 512):
        c1 = min(c0 + 512, n)
        k.op("t", "matmul", out=psum[:, 0:c1 - c0], lhsT=C.ones_row_f[:],
             rhs=src_row[0:1, c0:c1], start=True, stop=True)
        tt = k.inc("t")
        k.wait("s", tt)
        k.op("s", "copy", out=dst[:, c0:c1], in_=psum[:, 0:c1 - c0])
        ts = k.inc("s")
        k.wait("t", ts)
    return ts


def emit_ln_tile(k, x_tile, xn_out, stats, eps=EPS_LN, extra_out=None):
    v, s = "v", "s"
    k.op(v, "bn_stats", out=stats[:, 0:6], in_=x_tile[:, 0:512])
    k.op(v, "bn_stats", out=stats[:, 6:12], in_=x_tile[:, 512:1024])
    t1 = k.inc(v)
    k.wait(v, t1)
    k.op(v, "bn_aggr", out=stats[:, 12:14], in_=stats[:, 0:12])
    t2 = k.inc(v)
    k.wait(v, t2)
    k.op(v, "tensor_single_scalar", out=stats[:, 14:15], in_=stats[:, 12:13], scalar=-1.0, op=ALU.mult)
    tneg = k.inc(v)
    k.wait(s, t2)
    k.op(s, "activation", out=stats[:, 15:16], in_=stats[:, 13:14], func=AF.Ln, bias=k.C.eps_ln[:], scale=1.0)
    t3 = k.inc(s)
    k.wait(s, t3)
    k.op(s, "activation", out=stats[:, 16:17], in_=stats[:, 15:16], func=AF.Exp, scale=-0.5)
    t4 = k.inc(s)
    k.wait(v, t4, tneg)
    k.op(v, "tensor_scalar", out=xn_out, in0=x_tile[:, 0:D], scalar1=stats[:, 14:15],
         scalar2=stats[:, 16:17], op0=ALU.add, op1=ALU.mult)
    tk = k.inc(v)
    if extra_out is not None:
        k.op(v, "tensor_scalar", out=extra_out, in0=x_tile[:, 0:D], scalar1=stats[:, 14:15],
             scalar2=stats[:, 16:17], op0=ALU.add, op1=ALU.mult)
        tk = k.inc(v)
    return tk


def emit_transpose_tile(k, src_tile, dst_fn, psA, psB, dep=None, evac_eng="s"):
    C = k.C
    tok = None
    for dc in range(DC):
        ps = psA if dc % 2 == 0 else psB
        k.wait("t", dep, k.bank_guard.get(id(ps)))
        k.op("t", "transpose", out=ps[:], in_=src_tile[:, dc * 128:(dc + 1) * 128], identity=C.ident_b[:])
        tt = k.inc("t")
        k.wait(evac_eng, tt)
        k.op(evac_eng, "copy" if evac_eng == "s" else "tensor_copy", out=dst_fn(dc), in_=ps[:])
        tok = k.inc(evac_eng)
        k.bank_guard[id(ps)] = tok
    return tok


def emit_l0(k, W, x_in, x1_out, dbg=None):
    nc = k.nc
    C = k.C
    l0es = contextlib.ExitStack()
    es = l0es
    S = lambda *a, **kw: es.enter_context(nc.sbuf_tensor(*a, **kw))

    x_r = x_in.rearrange("(n p) d -> p n d", p=128)
    x1_r = x1_out.rearrange("(n p) d -> p n d", p=128)

    xn_fm = S("l0_xn_fm", [128, DC * T], BF16)
    s_tm = S("l0_s_tm", [128, NT * D], BF16)
    brow = S("l0_brow", [1, D], F32)

    with contextlib.ExitStack() as pes:
        PS = lambda *a, **kw: pes.enter_context(nc.psum_tensor(*a, **kw))
        SS = lambda *a, **kw: pes.enter_context(nc.sbuf_tensor(*a, **kw))
        ps_a = PS("pa_ps", [128, 512], F32)
        wbig = SS("pa_wbig", [128, FF], F32)
        wint = SS("pa_wint", [128, FF], I32)
        wtb = SS("pa_wtb", [128, FF], BF16)
        acc = SS("pa_acc", [128, 2], F32)
        tots = SS("pa_tots", [1, 2], F32)
        st = SS("pa_st", [128, 1], F32)
        rs1 = SS("pa_rs1", [128, 1], F32)
        rs2 = SS("pa_rs2", [128, 1], F32)

        for wi, (wnm, rows, rsdst) in enumerate([("w1T", D, rs1), ("w2T", FF, rs2)]):
            nr = rows // 128
            per = FF if wnm == "w1T" else D
            k.op("v", "memset", acc[:, wi:wi + 1], 0.0)
            k.inc("v")
            for r in range(nr):
                dd = k.dma("q", out=wbig[:, 0:per],
                           in_=W[wnm].rearrange("(c p) f -> p c f", p=128)[:, r, :],
                           wait=(k.tok("v"),))
                k.wait("v", dd)
                k.op("v", "reduce_sum", out=st[:, 0:1], in_=wbig[:, 0:per], axis=AX.X,
                     apply_absolute_value=True)
                tv = k.inc("v")
                k.wait("v", tv)
                k.op("v", "tensor_tensor", out=acc[:, wi:wi + 1], in0=acc[:, wi:wi + 1],
                     in1=st[:, 0:1], op=ALU.add)
                k.inc("v")
            tv = k.inc("v")
            k.wait("t", tv)
            k.op("t", "matmul", out=ps_a[0:1, 0:1], lhsT=acc[:, wi:wi + 1], rhs=C.ones_col_f[:],
                 start=True, stop=True)
            tt = k.inc("t")
            k.wait("v", tt)
            k.op("v", "reciprocal", out=tots[0:1, wi:wi + 1], in_=ps_a[0:1, 0:1])
            tv = k.inc("v")
            k.wait("v", tv)
            k.op("v", "tensor_single_scalar", out=tots[0:1, wi:wi + 1], in_=tots[0:1, wi:wi + 1],
                 scalar=float(FF * D), op=ALU.mult)
            tv = k.inc("v")
            k.wait("t", tv)
            k.op("t", "matmul", out=ps_a[:, 0:1], lhsT=C.ones_row_f[:], rhs=tots[0:1, wi:wi + 1],
                 start=True, stop=True)
            tt = k.inc("t")
            k.wait("s", tt)
            k.op("s", "copy", out=rsdst[:], in_=ps_a[:, 0:1])
            ts = k.inc("s")
            k.wait("t", ts)

        for wnm, rows, rsdst, scr in [("w1T", D, rs1, "w1t_scr"), ("w2T", FF, rs2, "w2t_scr")]:
            nr = rows // 128
            per = FF if wnm == "w1T" else D
            for r in range(nr):
                dd = k.dma("q", out=wbig[:, 0:per],
                           in_=W[wnm].rearrange("(c p) f -> p c f", p=128)[:, r, :],
                           wait=(k.tok("v"),))
                k.wait("v", dd)
                k.op("v", "tensor_scalar", out=wint[:, 0:per], in0=wbig[:, 0:per],
                     scalar1=rsdst[:], scalar2=None, op0=ALU.mult, op1=ALU.bypass)
                tv = k.inc("v")
                k.wait("v", tv)
                k.op("v", "tensor_scalar", out=wtb[:, 0:per], in0=wint[:, 0:per],
                     scalar1=1, scalar2=-1, op0=ALU.min, op1=ALU.max)
                tv = k.inc("v")
                dd = k.dma("q", out=W[scr].rearrange("(c p) f -> p c f", p=128)[:, r, :],
                           in_=wtb[:, 0:per], wait=(tv,))
                k.wait("v", dd)
        k.fence_all()

    with contextlib.ExitStack() as pes:
        PS = lambda *a, **kw: pes.enter_context(nc.psum_tensor(*a, **kw))
        SS = lambda *a, **kw: pes.enter_context(nc.sbuf_tensor(*a, **kw))
        ps_ta = PS("pb_ps_ta", [128, 128], BF16)
        ps_tb = PS("pb_ps_tb", [128, 128], BF16)
        xt = [SS(f"pb_xt{i}", [128, D], F32) for i in range(2)]
        xnt = [SS(f"pb_xnt{i}", [128, D], BF16) for i in range(2)]
        stats = SS("pb_stats", [128, 34], F32)
        prev = [None, None]
        for i in range(NT):
            p = i % 2
            dd = k.dma("q", out=xt[p][:, :], in_=x_r[:, i, :], wait=(prev[p],))
            k.wait("v", dd)
            tln = emit_ln_tile(k, xt[p], xnt[p][:, :], stats[:, p * 17:p * 17 + 17])
            tlast = emit_transpose_tile(
                k, xnt[p], lambda dc, i=i: xn_fm[:, dc * T + i * 128: dc * T + (i + 1) * 128],
                ps_ta, ps_tb, dep=tln)
            prev[p] = tlast
        k.fence_all()

    if dbg and "xn0_fm" in dbg:
        k.dma("q", out=dbg["xn0_fm"], in_=xn_fm.ap())
        k.fence_all()

    attn_es = contextlib.ExitStack()
    with contextlib.ExitStack() as pes:
        PS = lambda *a, **kw: pes.enter_context(nc.psum_tensor(*a, **kw))
        SS = lambda *a, **kw: pes.enter_context(nc.sbuf_tensor(*a, **kw))
        q_ext = attn_es.enter_context(nc.sbuf_tensor("l0_q_ext", [128, DC * (T + 16)], BF16))
        k_fm = attn_es.enter_context(nc.sbuf_tensor("l0_k_fm", [128, DC * T], BF16))
        v_tm = attn_es.enter_context(nc.sbuf_tensor("l0_v_tm", [128, NT * H0 * (HD0 + 1)], BF16))
        k.q_ext, k.k_fm, k.v_tm = q_ext, k_fm, v_tm
        ps_a = PS("pc_ps_a", [128, 512], F32)
        ps_b = PS("pc_ps_b", [128, 512], F32)
        wqs = [SS(f"pc_wqs{i}", [128, DC * 128], BF16) for i in range(2)]
        wvs = [SS(f"pc_wvs{i}", [128, DC * 512], BF16) for i in range(2)]
        bqkv_c = SS("pc_bqkv_c", [128, 16], F32)
        bv_bc = SS("pc_bv_bc", [128, D], F32)

        k.dma("q", out=bqkv_c[:, :], in_=W["bqkv_qk_c"])
        dv_ = k.dma("q", out=brow[:, :], in_=W["bqkv_v_row"])
        k.wait("t", dv_)
        emit_bcast_row(k, bv_bc, brow[0:1, 0:D], ps_a, D)

        qv = q_ext.ap().rearrange("p (c n) -> p c n", c=DC)
        k.op("v", "memset", qv[:, :, 0:8], 0.0)
        k.op("v", "memset", qv[:, :, 8 + T:16 + T], 0.0)
        vv = v_tm.ap().rearrange("p (n h c) -> p n h c", n=NT, h=H0)
        k.op("v", "memset", vv[:, :, :, HD0:HD0 + 1], 1.0)
        k.inc("v")

        wq_r = W["w_qkvT"].rearrange("(c p) f -> p c f", p=128)
        oc_done = {}
        for oc in range(16):
            wb = wqs[oc % 2]
            dd = k.dma("g", out=wb.ap().rearrange("p (c f) -> p c f", c=DC),
                       in_=wq_r[:, :, oc * 128:(oc + 1) * 128], wait=(oc_done.get(oc - 2),))
            k.wait("t", dd)
            for tsb in range(4):
                ps = ps_a if tsb % 2 == 0 else ps_b
                for dc in range(DC):
                    k.op("t", "matmul", out=ps[:, 0:512],
                         lhsT=wb[:, dc * 128:(dc + 1) * 128],
                         rhs=xn_fm[:, dc * T + tsb * 512: dc * T + (tsb + 1) * 512],
                         start=(dc == 0), stop=(dc == DC - 1))
                tt = k.inc("t")
                k.wait("s", tt)
                if oc < 8:
                    dst = q_ext[:, oc * (T + 16) + 8 + tsb * 512: oc * (T + 16) + 8 + (tsb + 1) * 512]
                else:
                    dst = k_fm[:, (oc - 8) * T + tsb * 512: (oc - 8) * T + (tsb + 1) * 512]
                k.op("s", "activation", out=dst, in_=ps[:, 0:512], func=AF.Identity,
                     bias=bqkv_c[:, oc:oc + 1], scale=1.0)
                ts = k.inc("s")
                k.wait("t", ts)
            oc_done[oc] = k.tok("s")
        half_done = {}
        for half in range(2):
            wb = wvs[half % 2]
            dd = k.dma("g", out=wb.ap().rearrange("p (c f) -> p c f", c=DC),
                       in_=wq_r[:, :, 2048 + half * 512: 2048 + (half + 1) * 512])
            k.wait("t", dd)
            for tt_i in range(NT):
                ps = ps_a if tt_i % 2 == 0 else ps_b
                for dc in range(DC):
                    k.op("t", "matmul", out=ps[:, 0:512],
                         lhsT=xn_fm[:, dc * T + tt_i * 128: dc * T + (tt_i + 1) * 128],
                         rhs=wb[:, dc * 512:(dc + 1) * 512],
                         start=(dc == 0), stop=(dc == DC - 1))
                tmm = k.inc("t")
                k.wait("v", tmm)
                k.op("v", "tensor_tensor", out=vv[:, tt_i, half * 8:(half + 1) * 8, 0:HD0],
                     in0=ps.ap()[:, 0:512].rearrange("p (h c) -> p h c", h=8),
                     in1=bv_bc.ap()[:, half * 512:(half + 1) * 512].rearrange("p (h c) -> p h c", h=8),
                     op=ALU.add)
                tv = k.inc("v")
                k.wait("t", tv)
        k.fence_all()

    if dbg and "qkv" in dbg:
        k.dma("q", out=dbg["q_ext"], in_=k.q_ext.ap())
        k.dma("q", out=dbg["k_fm"], in_=k.k_fm.ap())
        k.dma("q", out=dbg["v_tm"], in_=k.v_tm.ap())
        k.fence_all()

    with contextlib.ExitStack() as pes:
        PS = lambda *a, **kw: pes.enter_context(nc.psum_tensor(*a, **kw))
        SS = lambda *a, **kw: pes.enter_context(nc.sbuf_tensor(*a, **kw))
        q_ext, k_fm, v_tm = k.q_ext, k.k_fm, k.v_tm
        woT_b = SS("pd_woT_b", [128, DC * D], BF16)
        bout_bc = SS("pd_bout_bc", [128, D], F32)
        E_sb = SS("pd_E", [128, 3 * H0 * WIN], BF16)
        ctx_fm = SS("pd_ctx", [128, DC * 128], BF16)
        cc_sb = SS("pd_cc", [128, 2 * 128], BF16)
        zrow = SS("pd_zrow", [1, 2 * 128], F32)
        ps_sc = [PS(f"pd_ps_sc{i}", [128, 512], F32) for i in range(2)]
        ps_cx = [PS(f"pd_ps_cx{i}", [128, 512], F32) for i in range(2)]
        ps_zb = [PS(f"pd_ps_zb{i}", [128, 128], F32) for i in range(2)]
        ps_o = [PS(f"pd_ps_o{i}", [128, 512], F32) for i in range(2)]

        for dc in range(DC):
            k.dma("g", out=woT_b[:, dc * D:(dc + 1) * D],
                  in_=W["w_outT"].rearrange("(c p) f -> p c f", p=128)[:, dc, :])
        dwo = ("d", "g", k.dcnt["g"])
        d2 = k.dma("q", out=brow[:, :], in_=W["b_out_row"])
        k.wait("t", d2)
        emit_bcast_row(k, bout_bc, brow[0:1, 0:D], ps_o[0], D)

        vt = v_tm.ap().rearrange("p (n h c) -> p n h c", n=NT, h=H0)

        def qoff(h):
            return (h % 2) * 64, h // 2

        exp_toks = {}
        sc_ctr = [0]

        def emit_scores(kt):
            slot = kt % 3
            tvlast = None
            for h in range(H0):
                pb, ch = qoff(h)
                ps = ps_sc[h % 2]
                sc = sc_ctr[0]
                sc_ctr[0] += 1
                k.wait("t", exp_toks.get(sc - 2))
                k.op("t", "matmul", out=ps[:, 0:WIN],
                     lhsT=k_fm[pb:pb + HD0, ch * T + kt * 128: ch * T + (kt + 1) * 128],
                     rhs=q_ext[pb:pb + HD0, ch * (T + 16) + kt * 128: ch * (T + 16) + kt * 128 + WIN],
                     start=True, stop=True)
                tt = k.inc("t")
                k.wait("s", tt)
                dst = E_sb[:, (slot * H0 + h) * WIN:(slot * H0 + h + 1) * WIN]
                k.op("s", "activation", out=dst, in_=ps[:, 0:WIN], func=AF.Exp, bias=C.m12[:], scale=0.125)
                exp_toks[sc] = k.inc("s")
                k.wait("v", exp_toks[sc])
                k.op("v", "tensor_tensor", out=dst, in0=dst, in1=C.band_b[:], op=ALU.mult)
                tvlast = k.inc("v")
            return tvlast

        zb_toks = {}
        cc_toks = {}
        gh_ctr = [0]

        def ctx_grp_mm(qt, grp, depE):
            ps = ps_cx[grp % 2]
            prevg = (qt * 4 + grp) - 2
            k.wait("t", depE, cc_toks.get(("grp", prevg)))
            for hh in range(4):
                h = grp * 4 + hh
                cols = hh * 128
                k.op("t", "matmul", out=ps[0:HD0 + 1, cols:cols + 128],
                     lhsT=vt[:, qt, h, :],
                     rhs=E_sb[:, ((qt % 3) * H0 + h) * WIN + 8:((qt % 3) * H0 + h) * WIN + 136],
                     start=True, stop=True, skip_group_check=True)
                if qt - 1 >= 0:
                    sl = (qt - 1) % 3
                    k.op("t", "matmul", out=ps[0:HD0 + 1, cols:cols + 8],
                         lhsT=vt[:, qt - 1, h, :],
                         rhs=E_sb[:, (sl * H0 + h) * WIN + 136:(sl * H0 + h) * WIN + 144],
                         start=False, stop=True, skip_group_check=True)
                if qt + 1 < NT:
                    sl = (qt + 1) % 3
                    k.op("t", "matmul", out=ps[0:HD0 + 1, cols + 120:cols + 128],
                         lhsT=vt[:, qt + 1, h, :],
                         rhs=E_sb[:, (sl * H0 + h) * WIN:(sl * H0 + h) * WIN + 8],
                         start=False, stop=True, skip_group_check=True)
            return k.inc("t")

        def ctx_grp_evac(qt, grp, tmm):
            ps = ps_cx[grp % 2]
            for hh in range(4):
                h = grp * 4 + hh
                cols = hh * 128
                pb, ch = qoff(h)
                gh = gh_ctr[0]
                gh_ctr[0] += 1
                k.wait("v", tmm, zb_toks.get(gh - 2))
                k.op("v", "reciprocal", out=zrow[0:1, (h % 2) * 128:(h % 2) * 128 + 128],
                     in_=ps[HD0:HD0 + 1, cols:cols + 128])
                tz = k.inc("v")
                k.wait("t", tz, cc_toks.get(gh - 2))
                k.op("t", "matmul", out=ps_zb[h % 2][0:HD0, 0:128],
                     lhsT=C.ones_row_f[0:1, 0:HD0],
                     rhs=zrow[0:1, (h % 2) * 128:(h % 2) * 128 + 128],
                     start=True, stop=True, skip_group_check=True)
                zb_toks[gh] = k.inc("t")
                k.wait("s", tmm, cc_toks.get(gh - 2))
                k.op("s", "copy", out=cc_sb[pb:pb + HD0, (h % 2) * 128:(h % 2) * 128 + 128],
                     in_=ps[0:HD0, cols:cols + 128])
                tcc = k.inc("s")
                k.wait("v", tcc, zb_toks[gh])
                k.op("v", "tensor_tensor", out=ctx_fm[pb:pb + HD0, ch * 128:(ch + 1) * 128],
                     in0=cc_sb[pb:pb + HD0, (h % 2) * 128:(h % 2) * 128 + 128],
                     in1=ps_zb[h % 2][0:HD0, 0:128], op=ALU.mult)
                cc_toks[gh] = k.inc("v")
            cc_toks[("grp", qt * 4 + grp)] = cc_toks[gh_ctr[0] - 1]

        def emit_ctx(qt, depE):
            t0 = ctx_grp_mm(qt, 0, depE)
            t1 = ctx_grp_mm(qt, 1, depE)
            ctx_grp_evac(qt, 0, t0)
            t2 = ctx_grp_mm(qt, 2, depE)
            ctx_grp_evac(qt, 1, t1)
            t3 = ctx_grp_mm(qt, 3, depE)
            ctx_grp_evac(qt, 2, t2)
            ctx_grp_evac(qt, 3, t3)
            k.wait("t", cc_toks[gh_ctr[0] - 1], dwo)
            for half in range(2):
                ps = ps_o[half]
                for dc in range(DC):
                    k.op("t", "matmul", out=ps[:, 0:512],
                         lhsT=ctx_fm[:, dc * 128:(dc + 1) * 128],
                         rhs=woT_b[:, dc * D + half * 512: dc * D + (half + 1) * 512],
                         start=(dc == 0), stop=(dc == DC - 1))
                tt = k.inc("t")
                k.wait("v", tt)
                k.op("v", "tensor_tensor", out=s_tm[:, qt * D + half * 512: qt * D + (half + 1) * 512],
                     in0=ps[:, 0:512], in1=bout_bc[:, half * 512:(half + 1) * 512], op=ALU.add)
                tv = k.inc("v")
                k.wait("t", tv)

        tE = emit_scores(0)
        tE = emit_scores(1)
        for qt in range(NT):
            if qt + 1 < NT and qt + 1 >= 2:
                tE = emit_scores(qt + 1)
            emit_ctx(qt, tE)
        k.fence_all()

    attn_es.close()

    if dbg and "s_tm" in dbg:
        k.dma("q", out=dbg["s_tm"], in_=s_tm.ap())
        k.fence_all()

    with contextlib.ExitStack() as pes:
        PS = lambda *a, **kw: pes.enter_context(nc.psum_tensor(*a, **kw))
        SS = lambda *a, **kw: pes.enter_context(nc.sbuf_tensor(*a, **kw))
        hg = SS("pe_hg", [128, 32 * 512], BF16)
        w1t_b = [SS(f"pe_w1t{i}", [128, DC * 128], BF16) for i in range(2)]
        w2t_b = [SS(f"pe_w2t{i}", [128, D], BF16) for i in range(2)]
        gtmp = SS("pe_gtmp", [128, D], F32)
        g0 = SS("pe_g0", [128, D], BF16)
        yt = SS("pe_yt", [128, D], F32)
        b1_c = SS("pe_b1_c", [128, 32], F32)
        b2_bc = SS("pe_b2_bc", [128, D], F32)
        ps_h = [PS(f"pe_ps_h{i}", [128, 512], F32) for i in range(2)]
        ps_y6 = [PS(f"pe_ps_y{i}", [128, 512], F32) for i in range(6)]
        ps_y = ps_y6 + ps_h

        w1scr_r = W["w1t_scr"].rearrange("(c p) f -> p c f", p=128)
        w2scr_r = W["w2t_scr"].rearrange("(c p) f -> p c f", p=128)

        k.dma("q", out=b1_c[:, :], in_=W["b1_c"])
        d3 = k.dma("q", out=brow[:, :], in_=W["b2_row"])
        k.wait("t", d3)
        emit_bcast_row(k, b2_bc, brow[0:1, 0:D], ps_h[0], D)

        hload = {}
        for tsb in range(4):
            for ffc in range(32):
                wb = w1t_b[ffc % 2]
                dd = k.dma("q", out=wb.ap().rearrange("p (c f) -> p c f", c=DC),
                           in_=w1scr_r[:, :, ffc * 128:(ffc + 1) * 128],
                           wait=(hload.get(ffc - 2),))
                ps = ps_h[ffc % 2]
                k.wait("t", dd)
                for dc in range(DC):
                    k.op("t", "matmul", out=ps[:, 0:512],
                         lhsT=wb[:, dc * 128:(dc + 1) * 128],
                         rhs=xn_fm[:, dc * T + tsb * 512: dc * T + (tsb + 1) * 512],
                         start=(dc == 0), stop=(dc == DC - 1))
                tt = k.inc("t")
                hload[ffc] = tt
                k.wait("s", tt)
                k.op("s", "activation", out=hg[:, ffc * 512:(ffc + 1) * 512], in_=ps[:, 0:512],
                     func=AF.Gelu, bias=b1_c[:, ffc:ffc + 1], scale=1.0)
                ts = k.inc("s")
                k.wait("t", ts)
            w2load = {}
            for ffc in range(32):
                wb = w2t_b[ffc % 2]
                dd = k.dma("q", out=wb[:, :], in_=w2scr_r[:, ffc, :], wait=(w2load.get(ffc - 2),))
                k.wait("t", dd)
                for tt_i in range(4):
                    for half in range(2):
                        k.op("t", "matmul", out=ps_y[tt_i * 2 + half][:, 0:512],
                             lhsT=hg[:, ffc * 512 + tt_i * 128: ffc * 512 + (tt_i + 1) * 128],
                             rhs=wb[:, half * 512:(half + 1) * 512],
                             start=(ffc == 0), stop=(ffc == 31))
                w2load[ffc] = k.inc("t")
            tw2 = k.tok("t")
            for tt_i in range(4):
                gi = tsb * 4 + tt_i
                dd = k.dma("q", out=yt[:, :], in_=x_r[:, gi, :], wait=(k.tok("v"),))
                for half in range(2):
                    ps = ps_y[tt_i * 2 + half]
                    k.wait("v", tw2)
                    k.op("v", "tensor_tensor", out=gtmp[:, half * 512:(half + 1) * 512],
                         in0=ps[:, 0:512], in1=b2_bc[:, half * 512:(half + 1) * 512], op=ALU.add)
                tv = k.inc("v")
                k.wait("s", tv)
                k.op("s", "activation", out=g0[:, :], in_=gtmp[:, :], func=AF.Silu)
                ts = k.inc("s")
                k.wait("v", ts, dd)
                k.op("v", "tensor_tensor", out=g0[:, :], in0=g0[:, :],
                     in1=s_tm[:, gi * D:(gi + 1) * D], op=ALU.mult)
                tv = k.inc("v")
                k.wait("v", tv)
                k.op("v", "tensor_tensor", out=yt[:, :], in0=yt[:, :], in1=g0[:, :], op=ALU.add)
                tv = k.inc("v")
                k.dma("q", out=x1_r[:, gi, :], in_=yt[:, :], wait=(tv,))
                k.wait("v", ("d", "q", k.dcnt["q"]))
            k.wait("s", k.tok("t"))
        k.fence_all()
    l0es.close()


TS = T + 1


def emit_l1_ln(k, W, x1_in, xn1_fm, xn_lo_fm, dbg=None):
    nc, es = k.nc, k.es
    x1_r = x1_in.rearrange("(n p) d -> p n d", p=128)
    with contextlib.ExitStack() as pes:
        PS = lambda *a, **kw: pes.enter_context(nc.psum_tensor(*a, **kw))
        SS = lambda *a, **kw: pes.enter_context(nc.sbuf_tensor(*a, **kw))
        ps_ta = PS("pf_ps_ta", [128, 128], BF16)
        ps_tb = PS("pf_ps_tb", [128, 128], BF16)
        xt = [SS(f"pf_xt{i}", [128, D], F32) for i in range(2)]
        xf2 = [SS(f"pf_xf2{i}", [128, D], F32) for i in range(2)]
        xnt = [SS(f"pf_xnt{i}", [128, D], BF16) for i in range(2)]
        xerr = [SS(f"pf_xerr{i}", [128, D], BF16) for i in range(2)]
        stats = SS("pf_stats", [128, 34], F32)
        xnv = xn1_fm.ap().rearrange("p (c n) -> p c n", c=DC)
        k.op("v", "memset", xnv[:, :, 0:1], 0.0)
        k.inc("v")
        prev = [None, None]
        for i in range(NT):
            p = i % 2
            dd = k.dma("q", out=xt[p][:, :], in_=x1_r[:, i, :], wait=(prev[p],))
            k.wait("v", dd)
            tln = emit_ln_tile(k, xt[p], xnt[p][:, :], stats[:, p * 17:p * 17 + 17])
            k.wait("v", tln)
            k.op("v", "tensor_scalar", out=xf2[p][:, :], in0=xt[p][:, 0:D],
                 scalar1=stats[:, p * 17 + 14:p * 17 + 15],
                 scalar2=stats[:, p * 17 + 16:p * 17 + 17], op0=ALU.add, op1=ALU.mult)
            terr0 = k.inc("v")
            k.wait("v", terr0)
            k.op("v", "tensor_tensor", out=xerr[p][:, :], in0=xf2[p][:, :], in1=xnt[p][:, :],
                 op=ALU.subtract)
            terr = k.inc("v")
            emit_transpose_tile(
                k, xnt[p], lambda dc, i=i: xn1_fm[:, dc * TS + 1 + i * 128: dc * TS + 1 + (i + 1) * 128],
                ps_ta, ps_tb, dep=tln)
            t2 = emit_transpose_tile(
                k, xerr[p], lambda dc, i=i: xn_lo_fm[:, dc * T + i * 128: dc * T + (i + 1) * 128],
                ps_ta, ps_tb, dep=terr)
            prev[p] = t2
        k.fence_all()


def emit_l1_gate(k, W, xn1_fm, xn_lo_fm, dw_sb, dbg=None):
    nc = k.nc
    C = k.C
    with contextlib.ExitStack() as pes:
        PS = lambda *a, **kw: pes.enter_context(nc.psum_tensor(*a, **kw))
        SS = lambda *a, **kw: pes.enter_context(nc.sbuf_tensor(*a, **kw))
        gw_f = SS("pg_gw_f", [128, DC * NE], F32)
        gw_hi = SS("pg_gw_hi", [128, DC * NE], BF16)
        gw_lo = SS("pg_gw_lo", [128, DC * NE], BF16)
        gerr = SS("pg_gerr", [128, DC * NE], F32)
        lg = SS("pg_lg", [128, NE], F32)
        ab = SS("pg_ab", [128, NE], F32)
        u1 = SS("pg_u1", [128, NE], F32)
        m1 = SS("pg_m1", [128, NE], F32)
        e1 = SS("pg_e1", [128, NE], F32)
        x1s = SS("pg_x1s", [128, NE], F32)
        oh1 = SS("pg_oh1", [128, NE], F32)
        s2 = SS("pg_s2", [128, NE], F32)
        oh2 = SS("pg_oh2", [128, NE], F32)
        cols = SS("pg_cols", [128, 16], F32)
        ps_g = [PS(f"pg_ps{i}", [128, 512], F32) for i in range(2)]

        dd = k.dma("q", out=gw_f.ap().rearrange("p (c e) -> p c e", c=DC),
                   in_=W["gate_wT"].rearrange("(c p) e -> p c e", p=128))
        k.wait("v", dd)
        k.op("v", "tensor_copy", out=gw_hi[:, :], in_=gw_f[:, :])
        tv = k.inc("v")
        k.wait("v", tv)
        k.op("v", "tensor_tensor", out=gerr[:, :], in0=gw_f[:, :], in1=gw_hi[:, :], op=ALU.subtract)
        tv = k.inc("v")
        k.wait("v", tv)
        k.op("v", "tensor_copy", out=gw_lo[:, :], in_=gerr[:, :])
        tglo = k.inc("v")
        k.wait("t", tglo)

        prev_read = {}
        for ti in range(NT):
            ps = ps_g[ti % 2]
            k.wait("t", prev_read.get(ti - 2))
            nmm = 3 * DC
            idx = 0
            for dc in range(DC):
                hi_x = xn1_fm[:, dc * TS + 1 + ti * 128: dc * TS + 1 + (ti + 1) * 128]
                lo_x = xn_lo_fm[:, dc * T + ti * 128: dc * T + (ti + 1) * 128]
                hi_w = gw_hi[:, dc * NE:(dc + 1) * NE]
                lo_w = gw_lo[:, dc * NE:(dc + 1) * NE]
                for lx, lw in ((hi_x, hi_w), (lo_x, hi_w), (hi_x, lo_w)):
                    k.op("t", "matmul", out=ps[:, 0:NE], lhsT=lx, rhs=lw,
                         start=(idx == 0), stop=(idx == nmm - 1))
                    idx += 1
            tt = k.inc("t")
            v = "v"
            k.wait(v, tt)
            k.op(v, "tensor_copy", out=lg[:, :], in_=ps[:, 0:NE])
            tl = k.inc(v)
            prev_read[ti] = tl
            k.wait(v, tl)
            k.op(v, "reduce_max", out=cols[:, 0:1], in_=lg[:, :], axis=AX.X)
            t_ = k.inc(v); k.wait(v, t_)
            k.op(v, "tensor_single_scalar", out=ab[:, :], in_=lg[:, :], scalar=-1.0, op=ALU.mult)
            t_ = k.inc(v); k.wait(v, t_)
            k.op(v, "tensor_tensor", out=ab[:, :], in0=ab[:, :], in1=lg[:, :], op=ALU.max)
            t_ = k.inc(v); k.wait(v, t_)
            k.op(v, "tensor_scalar", out=oh1[:, :], in0=lg[:, :], scalar1=cols[:, 0:1],
                 scalar2=None, op0=ALU.is_equal, op1=ALU.bypass)
            t_ = k.inc(v); k.wait(v, t_)
            k.op(v, "tensor_scalar", out=u1[:, :], in0=lg[:, :], scalar1=cols[:, 0:1],
                 scalar2=-1.0, op0=ALU.subtract, op1=ALU.mult)
            t_ = k.inc(v); k.wait(v, t_)
            k.op(v, "tensor_scalar", out=m1[:, :], in0=ab[:, :], scalar1=cols[:, 0:1],
                 scalar2=-JIT2, op0=ALU.max, op1=ALU.mult)
            t_ = k.inc(v); k.wait(v, t_)
            k.op(v, "tensor_tensor", out=m1[:, :], in0=u1[:, :], in1=m1[:, :], op=ALU.add)
            t_ = k.inc(v); k.wait(v, t_)
            k.op(v, "tensor_single_scalar", out=m1[:, :], in_=m1[:, :], scalar=0.0, op=ALU.is_gt)
            t_ = k.inc(v); k.wait(v, t_)
            k.op(v, "scalar_tensor_tensor", out=e1[:, :], in0=m1[:, :], scalar=NEG, in1=lg[:, :],
                 op0=ALU.mult, op1=ALU.add)
            t_ = k.inc(v)
            k.op(v, "tensor_single_scalar", out=cols[:, 1:2], in_=cols[:, 0:1], scalar=-1.0, op=ALU.mult)
            t_ = k.inc(v); k.wait("s", t_)
            k.op("s", "activation", out=x1s[:, :], in_=e1[:, :], func=AF.Exp, bias=cols[:, 1:2], scale=1.0)
            ts_ = k.inc("s")
            k.wait(v, ts_)
            k.op(v, "reduce_sum", out=cols[:, 3:4], in_=x1s[:, :], axis=AX.X)
            t_ = k.inc(v); k.wait(v, t_)
            k.op(v, "tensor_tensor", out=e1[:, :], in0=x1s[:, :], in1=oh1[:, :], op=ALU.mult)
            t_ = k.inc(v); k.wait(v, t_)
            k.op(v, "reduce_sum", out=cols[:, 4:5], in_=e1[:, :], axis=AX.X)
            t_ = k.inc(v); k.wait(v, t_)
            k.op(v, "reciprocal", out=cols[:, 5:6], in_=cols[:, 3:4])
            t_ = k.inc(v); k.wait(v, t_)
            k.op(v, "tensor_tensor", out=cols[:, 5:6], in0=cols[:, 4:5], in1=cols[:, 5:6], op=ALU.mult)
            t_ = k.inc(v); k.wait(v, t_)
            k.op(v, "scalar_tensor_tensor", out=s2[:, :], in0=oh1[:, :], scalar=NEG, in1=lg[:, :],
                 op0=ALU.mult, op1=ALU.add)
            t_ = k.inc(v); k.wait(v, t_)
            k.op(v, "reduce_max", out=cols[:, 8:9], in_=s2[:, :], axis=AX.X)
            t_ = k.inc(v); k.wait(v, t_)
            k.op(v, "tensor_scalar", out=oh2[:, :], in0=s2[:, :], scalar1=cols[:, 8:9],
                 scalar2=None, op0=ALU.is_equal, op1=ALU.bypass)
            t_ = k.inc(v); k.wait(v, t_)
            k.op(v, "tensor_scalar", out=u1[:, :], in0=lg[:, :], scalar1=cols[:, 8:9],
                 scalar2=-1.0, op0=ALU.subtract, op1=ALU.mult)
            t_ = k.inc(v); k.wait(v, t_)
            k.op(v, "tensor_scalar", out=m1[:, :], in0=ab[:, :], scalar1=cols[:, 8:9],
                 scalar2=-JIT2, op0=ALU.max, op1=ALU.mult)
            t_ = k.inc(v); k.wait(v, t_)
            k.op(v, "tensor_tensor", out=m1[:, :], in0=u1[:, :], in1=m1[:, :], op=ALU.add)
            t_ = k.inc(v); k.wait(v, t_)
            k.op(v, "tensor_single_scalar", out=m1[:, :], in_=m1[:, :], scalar=0.0, op=ALU.is_gt)
            t_ = k.inc(v); k.wait(v, t_)
            k.op(v, "scalar_tensor_tensor", out=e1[:, :], in0=m1[:, :], scalar=NEG, in1=s2[:, :],
                 op0=ALU.mult, op1=ALU.add)
            t_ = k.inc(v)
            k.op(v, "tensor_single_scalar", out=cols[:, 9:10], in_=cols[:, 8:9], scalar=-1.0, op=ALU.mult)
            t_ = k.inc(v); k.wait("s", t_)
            k.op("s", "activation", out=x1s[:, :], in_=e1[:, :], func=AF.Exp, bias=cols[:, 9:10], scale=1.0)
            ts_ = k.inc("s")
            k.wait(v, ts_)
            k.op(v, "reduce_sum", out=cols[:, 11:12], in_=x1s[:, :], axis=AX.X)
            t_ = k.inc(v); k.wait(v, t_)
            k.op(v, "tensor_tensor", out=e1[:, :], in0=x1s[:, :], in1=oh2[:, :], op=ALU.mult)
            t_ = k.inc(v); k.wait(v, t_)
            k.op(v, "reduce_sum", out=cols[:, 12:13], in_=e1[:, :], axis=AX.X)
            t_ = k.inc(v); k.wait(v, t_)
            k.op(v, "reciprocal", out=cols[:, 13:14], in_=cols[:, 11:12])
            t_ = k.inc(v); k.wait(v, t_)
            k.op(v, "tensor_tensor", out=cols[:, 13:14], in0=cols[:, 12:13], in1=cols[:, 13:14], op=ALU.mult)
            t_ = k.inc(v); k.wait(v, t_)
            k.op(v, "tensor_scalar", out=oh1[:, :], in0=oh1[:, :], scalar1=cols[:, 5:6],
                 scalar2=None, op0=ALU.mult, op1=ALU.bypass)
            t_ = k.inc(v); k.wait(v, t_)
            k.op(v, "tensor_scalar", out=oh2[:, :], in0=oh2[:, :], scalar1=cols[:, 13:14],
                 scalar2=None, op0=ALU.mult, op1=ALU.bypass)
            t_ = k.inc(v); k.wait(v, t_)
            k.op(v, "tensor_tensor", out=dw_sb[:, ti * NE:(ti + 1) * NE], in0=oh1[:, :],
                 in1=oh2[:, :], op=ALU.add)
            k.inc(v)
        k.fence_all()


def emit_l1_moe(k, W, xn1_fm, dw_sb, ymoe_out, dbg=None):
    nc = k.nc
    C = k.C
    with contextlib.ExitStack() as pes:
        PS = lambda *a, **kw: pes.enter_context(nc.psum_tensor(*a, **kw))
        SS = lambda *a, **kw: pes.enter_context(nc.sbuf_tensor(*a, **kw))
        Y_sb = SS("ph_Y", [128, NT * D], F32)
        act = SS("ph_act", [128, 16 * 1024], BF16)
        w2_sb = SS("ph_w2", [128, 16 * D], BF16)
        w13 = [SS(f"ph_w13_{i}", [128, DC * 128], BF16) for i in range(4)]
        stmp = SS("ph_stmp", [128, 512], BF16)
        ps_h1 = [PS(f"ph_ps_h1{i}", [128, 512], F32) for i in range(2)]
        ps_h3 = [PS(f"ph_ps_h3{i}", [128, 512], F32) for i in range(2)]
        ps_y = [PS(f"ph_ps_y{i}", [128, 512], F32) for i in range(4)]

        k.op("v", "memset", Y_sb[:, :], 0.0)
        k.inc("v")
        w1r = W["moe_w1T"].rearrange("e (c p) f -> p e c f", p=128)
        w3r = W["moe_w3T"].rearrange("e (c p) f -> p e c f", p=128)
        w2r = W["moe_w2T"].rearrange("e (c p) f -> p e c f", p=128)

        wtoks = {}
        ytoks = {}
        ypsum_tok = {}
        yct = [0]
        fct = [0]
        for e in range(MOE_EXPERTS):
            dw2 = k.dma("g", out=w2_sb.ap().rearrange("p (c f) -> p c f", c=16),
                        in_=w2r[:, e, :, :], wait=(ytoks.get(e - 1),))
            if MOE_FENCE_E:
                k.fence_all()
            for tsb in range(2):
                tb0 = tsb * 1024
                if MOE_FENCE_TSB and tsb == 1:
                    k.fence_all()
                k.wait("v", k.tok("t"))
                k.wait("s", k.tok("t"))
                for ffc in range(16):
                    fc = fct[0]
                    fct[0] += 1
                    b1 = w13[(fc % 2) * 2]
                    b3 = w13[(fc % 2) * 2 + 1]
                    dd1 = k.dma("g", out=b1.ap().rearrange("p (c f) -> p c f", c=DC),
                                in_=w1r[:, e, :, ffc * 128:(ffc + 1) * 128],
                                wait=(wtoks.get(fc - 2),))
                    dd3 = k.dma("g", out=b3.ap().rearrange("p (c f) -> p c f", c=DC),
                                in_=w3r[:, e, :, ffc * 128:(ffc + 1) * 128])
                    k.wait("t", dd1, dd3)
                    for sub in range(2):
                        c0 = tb0 + sub * 512
                        ph1 = ps_h1[sub]
                        ph3 = ps_h3[sub]
                        for dc in range(DC):
                            rhsx = xn1_fm.ap().rearrange("p (c n) -> p c n", c=DC)[:, dc, 1 + c0:1 + c0 + 512]
                            k.op("t", "matmul", out=ph1[:, 0:512],
                                 lhsT=b1[:, dc * 128:(dc + 1) * 128], rhs=rhsx,
                                 start=(dc == 0), stop=(dc == DC - 1))
                        for dc in range(DC):
                            rhsx = xn1_fm.ap().rearrange("p (c n) -> p c n", c=DC)[:, dc, 1 + c0:1 + c0 + 512]
                            k.op("t", "matmul", out=ph3[:, 0:512],
                                 lhsT=b3[:, dc * 128:(dc + 1) * 128], rhs=rhsx,
                                 start=(dc == 0), stop=(dc == DC - 1))
                        tmm = k.inc("t")
                        k.wait("s", tmm)
                        k.op("s", "activation", out=stmp[:, 0:512], in_=ph1[:, 0:512], func=AF.Silu)
                        ts_ = k.inc("s")
                        k.wait("v", ts_)
                        k.op("v", "tensor_tensor", out=act[:, ffc * 1024 + sub * 512: ffc * 1024 + (sub + 1) * 512],
                             in0=stmp[:, 0:512], in1=ph3[:, 0:512], op=ALU.mult)
                        tv = k.inc("v")
                        k.wait("t", tv)
                        k.wait("s", tv)
                    if MOE_FENCE_H:
                        k.fence_all()
                    wtoks[fc] = k.tok("v")
                k.wait("t", dw2, wtoks[fct[0] - 1])
                for tt_loc in range(8):
                    ti = tsb * 8 + tt_loc
                    py = [ps_y[(tt_loc % 2) * 2], ps_y[(tt_loc % 2) * 2 + 1]]
                    yc = yct[0]
                    yct[0] += 1
                    k.wait("t", ypsum_tok.get(yc - 2))
                    for ffc in range(16):
                        for half in range(2):
                            k.op("t", "matmul", out=py[half][:, 0:512],
                                 lhsT=act[:, ffc * 1024 + tt_loc * 128: ffc * 1024 + (tt_loc + 1) * 128],
                                 rhs=w2_sb[:, ffc * D + half * 512: ffc * D + (half + 1) * 512],
                                 start=(ffc == 0), stop=(ffc == 15))
                    tym = k.inc("t")
                    k.wait("v", tym)
                    for half in range(2):
                        k.op("v", "scalar_tensor_tensor",
                             out=Y_sb[:, ti * D + half * 512: ti * D + (half + 1) * 512],
                             in0=py[half][:, 0:512], scalar=dw_sb[:, ti * NE + e: ti * NE + e + 1],
                             in1=Y_sb[:, ti * D + half * 512: ti * D + (half + 1) * 512],
                             op0=ALU.mult, op1=ALU.add)
                    tv = k.inc("v")
                    ypsum_tok[yc] = tv
                    if MOE_FENCE_Y:
                        k.fence_all()
            ytoks[e] = k.tok("t")
        k.fence_all()
        ym_r = ymoe_out.rearrange("(n p) d -> p n d", p=128)
        for i in range(NT):
            k.dma("q", out=ym_r[:, i, :], in_=Y_sb[:, i * D:(i + 1) * D])
        k.fence_all()


def emit_l1_scan(k, W, x1_in, ymoe_in, y_out, xn1_fm, dbg=None):
    nc, es = k.nc, k.es
    C = k.C
    x1_r = x1_in.rearrange("(n p) d -> p n d", p=128)
    ym_r = ymoe_in.rearrange("(n p) d -> p n d", p=128)
    y_r = y_out.rearrange("(n p) d -> p n d", p=128)
    CH = 128
    NCH = T // CH
    SUB = 32
    NSUB = CH // SUB
    CL = CH + 1

    with contextlib.ExitStack() as pes:
        PS = lambda *a, **kw: pes.enter_context(nc.psum_tensor(*a, **kw))
        SS = lambda *a, **kw: pes.enter_context(nc.sbuf_tensor(*a, **kw))
        vw = SS("pi_vw", [128, DC * VD], BF16)
        gw = SS("pi_gw", [128, DC * VD], BF16)
        ow = SS("pi_ow", [128, DC * D], BF16)
        rw = SS("pi_rw", [128, DC * KD], BF16)
        kw_ = SS("pi_kw", [128, DC * KD], BF16)
        ww1 = SS("pi_ww1", [128, DC * GD], BF16)
        ww2 = SS("pi_ww2", [64, KD], BF16)
        xlw1 = SS("pi_xlw1", [128, DC * RLO], BF16)
        xlw2 = SS("pi_xlw2", [32, R5], BF16)
        xow = SS("pi_xow", [33, 5 * D], BF16)
        mu_c = SS("pi_mu_c", [128, DC], F32)
        xlb2_c = SS("pi_xlb2", [128, 2], F32)
        wb_c = SS("pi_wb", [128, NH], F32)
        bon_c = SS("pi_bon", [128, NH], F32)
        gnw_c = SS("pi_gnw", [128, DC], F32)
        gnb_c = SS("pi_gnb", [128, DC], F32)
        Cl = SS("pi_Cl", [128, NH * CL], F32)
        S_f = SS("pi_S_f", [128, NH * DV], F32)
        S_b = SS("pi_S_b", [128, NH * DV], BF16)
        sl_f = SS("pi_sl_f", [128, NH * DV], F32)
        sl_b = SS("pi_sl_b", [128, NH * DV], BF16)
        d1 = SS("pi_d1", [128, DC * CH], BF16)
        xl = SS("pi_xl", [128, DC * CH], BF16)
        x1g_sb = SS("pi_x1g", [33, 5 * CH], BF16)
        x0t = SS("pi_x0t", [32, CH], BF16)
        mu_sb = SS("pi_mu_sb", [128, DC * CH], BF16)
        a_sb = [SS(f"pi_a{i}", [128, DC * CH], BF16) for i in range(5)]
        r_sb = SS("pi_r_sb", [128, KD], BF16)
        k_sb = SS("pi_k_sb", [128, KD], BF16)
        wt64 = SS("pi_wt64", [64, CH], BF16)
        lw_f = SS("pi_lw_f", [128, NH * CH], F32)
        vt_sb = SS("pi_vt", [128, VD], BF16)
        vtj_all = [SS(f"pi_vtj{j}", [32, NH * DV], BF16) for j in range(NSUB)]
        g_sb = SS("pi_g_sb", [128, DC * CH], BF16)
        rt_all = SS("pi_rt", [128, KD], BF16)
        rb_all = SS("pi_rb", [128, KD], BF16)
        fh2_sb = SS("pi_fh2", [128, KD], F32)
        f_sc = SS("pi_fsc", [128, 4 * NH * SUB], F32)
        rh_all = SS("pi_rh", [128, NH * SUB], BF16)
        kh_all = SS("pi_kh", [128, NH * SUB], BF16)
        k3_all = SS("pi_k3", [128, NH * SUB], BF16)
        rbk_all = SS("pi_rbk", [128, NH * SUB], BF16)
        P_sb = SS("pi_P", [32, NH * SUB], BF16)
        dgd_sb = SS("pi_dgd", [32, NH * SUB], BF16)
        dg_sb = SS("pi_dg", [1, NH * SUB], BF16)
        k3t_sb = SS("pi_k3t", [32, NH * 128], BF16)
        k2_sb = SS("pi_k2", [128, KD], BF16)
        k2t_sb = SS("pi_k2t", [128, KD], BF16)
        ecols = SS("pi_ecols", [128, 16], F32)
        o_sb = SS("pi_o_sb", [128, NH * DV], BF16)
        o2_sb = SS("pi_o2", [128, NH * DV], BF16)
        on_sb = SS("pi_on", [128, DC * CH], BF16)
        gnrow = SS("pi_gnrow", [1, 4 * 512], F32)
        xt_f = SS("pi_xt_f", [128, D], F32)
        ymt = SS("pi_ymt", [128, D], F32)
        g1t = SS("pi_g1t", [128, D], BF16)
        ps_big = PS("pi_ps_big", [128, 1024], F32)
        ps_pr = PS("pi_ps_pr", [128, 512], F32)
        ps_Oa = PS("pi_ps_Oa", [128, 512], F32)
        ps_Ob = PS("pi_ps_Ob", [128, 512], F32)
        ps_tp = PS("pi_ps_tp", [128, 512], BF16)
        ps_u = PS("pi_ps_u", [128, 1024], F32)

        def ldw(dst, nm, ncol, nch=DC):
            return k.dma("g", out=dst.ap().rearrange("p (c f) -> p c f", c=nch),
                         in_=W[nm].rearrange("(c p) f -> p c f", p=128))
        ldw(vw, "v_wT", VD)
        ldw(gw, "g_wT", VD)
        ldw(ow, "o_wT", D)
        ldw(rw, "r_wT", KD)
        ldw(kw_, "k_wT", KD)
        ldw(ww1, "w_w1T", GD)
        ldw(xlw1, "xl_w1T", RLO)
        k.dma("g", out=ww2[:, :], in_=W["w_w2T"])
        k.dma("g", out=xlw2[:, :], in_=W["xl_w2T"])
        k.dma("g", out=xow[:, :], in_=W["x_out_wTn"])
        k.dma("q", out=mu_c[:, :], in_=W["x_proj_mu_c"])
        k.dma("q", out=xlb2_c[:, :], in_=W["xlb2_cols"])
        k.dma("q", out=wb_c[:, :], in_=W["w_b_c"])
        k.dma("q", out=bon_c[:, :], in_=W["bonus_c"])
        k.dma("q", out=gnw_c[:, :], in_=W["gn_w_c"])
        k.dma("q", out=gnb_c[:, :], in_=W["gn_b_c"])
        k.fence_all()

        Cl3 = Cl.ap().rearrange("p (h n) -> p h n", h=NH)
        r3 = r_sb.ap().rearrange("p (h n) -> p h n", h=NH)
        k3v = k_sb.ap().rearrange("p (h n) -> p h n", h=NH)
        rb3 = rb_all.ap().rearrange("p (h n) -> p h n", h=NH)
        f4 = f_sc.ap().rearrange("p (a h s) -> p a h s", a=4, h=NH)
        rh3 = rh_all.ap().rearrange("p (h s) -> p h s", h=NH)
        kh3 = kh_all.ap().rearrange("p (h s) -> p h s", h=NH)
        rbk3 = rbk_all.ap().rearrange("p (h s) -> p h s", h=NH)
        fh23 = fh2_sb.ap().rearrange("p (h n) -> p h n", h=NH)

        k.op("v", "memset", Cl3[:, :, 0:1], 0.0)
        k.op("v", "memset", S_f[:, :], 0.0)
        k.op("v", "memset", S_b[:, :], 0.0)
        k.op("v", "memset", x1g_sb[32:33, :], 1.0)
        k.inc("v")
        k.fence_all()

        prev_tC = None
        for ci in range(NCH):
            c0 = ci * CH
            hcol = lambda dc: xn1_fm[:, dc * TS + 1 + c0: dc * TS + 1 + c0 + CH]
            scol = lambda dc: xn1_fm[:, dc * TS + c0: dc * TS + c0 + CH]
            k.wait("v", k.tok("t"))
            k.wait("g", k.tok("t"))
            k.wait("s", k.tok("t"))
            k.wait("s", k.tok("v"))
            k.wait("v", k.tok("s"))

            for dc in range(DC):
                k.op("v", "tensor_tensor", out=d1[:, dc * CH:(dc + 1) * CH], in0=scol(dc),
                     in1=hcol(dc), op=ALU.subtract)
            tv = k.inc("v")
            k.wait("v", tv)
            for dc in range(DC):
                k.op("v", "scalar_tensor_tensor", out=xl[:, dc * CH:(dc + 1) * CH],
                     in0=d1[:, dc * CH:(dc + 1) * CH], scalar=mu_c[:, dc:dc + 1],
                     in1=hcol(dc), op0=ALU.mult, op1=ALU.add)
            txl = k.inc("v")
            k.wait("t", txl)
            for dc in range(DC):
                k.op("t", "matmul", out=ps_pr[0:RLO, 0:CH], lhsT=xlw1[:, dc * RLO:(dc + 1) * RLO],
                     rhs=xl[:, dc * CH:(dc + 1) * CH], start=(dc == 0), stop=(dc == DC - 1))
            tt = k.inc("t")
            k.wait("s", tt)
            k.op("s", "activation", out=x0t[:, :], in_=ps_pr[0:RLO, 0:CH], func=AF.Tanh)
            ts_ = k.inc("s")
            k.wait("t", ts_)
            k.op("t", "matmul", out=ps_pr[0:96, 128:128 + CH], lhsT=xlw2[:, 0:96], rhs=x0t[:, :],
                 start=True, stop=True, skip_group_check=True)
            k.op("t", "matmul", out=ps_pr[0:64, 256:256 + CH], lhsT=xlw2[:, 96:160], rhs=x0t[:, :],
                 start=True, stop=True, skip_group_check=True)
            tt = k.inc("t")
            k.wait("s", tt)
            for n in range(3):
                k.op("s", "activation", out=x1g_sb[0:32, n * CH:(n + 1) * CH],
                     in_=ps_pr[32 * n:32 * (n + 1), 128:128 + CH], func=AF.Tanh,
                     bias=xlb2_c[32 * n:32 * (n + 1), 0:1], scale=1.0)
            for n in range(2):
                k.op("s", "activation", out=x1g_sb[0:32, (3 + n) * CH:(4 + n) * CH],
                     in_=ps_pr[32 * n:32 * (n + 1), 256:256 + CH], func=AF.Tanh,
                     bias=xlb2_c[32 * n:32 * (n + 1), 1:2], scale=1.0)
            ts_ = k.inc("s")
            k.wait("t", ts_)
            x1g = [x1g_sb[0:33, n * CH:(n + 1) * CH] for n in range(5)]
            for n in range(5):
                for dc in range(DC):
                    k.op("t", "matmul", out=ps_big[:, dc * 128:(dc + 1) * 128],
                         lhsT=xow[0:33, n * D + dc * 128: n * D + (dc + 1) * 128], rhs=x1g[n],
                         start=True, stop=True, skip_group_check=True)
                tt = k.inc("t")
                k.wait("s", tt)
                k.op("s", "copy", out=mu_sb[:, :], in_=ps_big[:, 0:DC * CH])
                ts_ = k.inc("s")
                k.wait("v", ts_)
                k.op("v", "tensor_tensor", out=a_sb[n][:, :], in0=mu_sb[:, :], in1=d1[:, :], op=ALU.mult)
                tv = k.inc("v")
                k.wait("g", tv)
                k.op("g", "tensor_tensor",
                     out=a_sb[n].ap().rearrange("p (c n) -> p c n", c=DC),
                     in0=a_sb[n].ap().rearrange("p (c n) -> p c n", c=DC),
                     in1=xn1_fm.ap().rearrange("p (c n) -> p c n", c=DC)[:, :, 1 + c0:1 + c0 + CH],
                     op=ALU.add)
                tg = k.inc("g")
                k.wait("t", tg)
                k.wait("s", tv)
            tlerp = k.tok("g")
            for (asrc, wsrc, dst) in ((a_sb[0], rw, r_sb), (a_sb[2], kw_, k_sb)):
                k.wait("t", tlerp)
                for kc in range(4):
                    for dc in range(DC):
                        k.op("t", "matmul", out=ps_pr[:, kc * 128:(kc + 1) * 128],
                             lhsT=wsrc[:, dc * KD + kc * 128: dc * KD + (kc + 1) * 128],
                             rhs=asrc[:, dc * CH:(dc + 1) * CH],
                             start=(dc == 0), stop=(dc == DC - 1))
                tt = k.inc("t")
                k.wait("s", tt)
                k.op("s", "copy", out=dst[:, :], in_=ps_pr[:, 0:KD])
                ts_ = k.inc("s")
                k.wait("t", ts_)
            tok_rk = k.tok("s")
            for dc in range(DC):
                k.op("t", "matmul", out=ps_pr[0:GD, 0:CH], lhsT=ww1[:, dc * GD:(dc + 1) * GD],
                     rhs=a_sb[1][:, dc * CH:(dc + 1) * CH], start=(dc == 0), stop=(dc == DC - 1))
            tt = k.inc("t")
            k.wait("s", tt)
            k.op("s", "activation", out=wt64[:, :], in_=ps_pr[0:GD, 0:CH], func=AF.Tanh)
            ts_ = k.inc("s")
            k.wait("t", ts_)
            for kc in range(4):
                k.op("t", "matmul", out=ps_pr[:, kc * 128:(kc + 1) * 128],
                     lhsT=ww2[:, kc * 128:(kc + 1) * 128], rhs=wt64[:, :],
                     start=True, stop=True)
            tt = k.inc("t")
            k.wait("s", tt)
            for kc in range(4):
                k.op("s", "activation", out=lw_f[:, kc * CH:(kc + 1) * CH],
                     in_=ps_pr[:, kc * 128:(kc + 1) * 128], func=AF.Exp,
                     bias=wb_c[:, kc:kc + 1], scale=1.0)
            ts_ = k.inc("s")
            k.wait("v", ts_)
            k.op("v", "tensor_single_scalar", out=lw_f[:, :], in_=lw_f[:, :], scalar=-1.0, op=ALU.mult)
            tv = k.inc("v")
            k.wait("v", tv, prev_tC)
            for h in range(NH):
                k.op("v", "tensor_tensor_scan", out=Cl3[:, h, 1:1 + CH],
                     data0=lw_f[:, h * CH:(h + 1) * CH], data1=lw_f[:, h * CH:(h + 1) * CH],
                     initial=Cl3[:, h, 0:1], op0=ALU.add, op1=ALU.bypass)
            tC = k.inc("v")
            k.wait("t", ts_, tC)
            for half in range(2):
                for dc in range(DC):
                    k.op("t", "matmul", out=ps_big[:, half * 512:(half + 1) * 512],
                         lhsT=a_sb[3][:, dc * CH:(dc + 1) * CH],
                         rhs=vw[:, dc * VD + half * 512: dc * VD + (half + 1) * 512],
                         start=(dc == 0), stop=(dc == DC - 1))
            tt = k.inc("t")
            k.wait("s", tt)
            k.op("s", "copy", out=vt_sb[:, :], in_=ps_big[:, 0:VD])
            tsv = k.inc("s")
            k.wait("t", tsv)
            for dc_o in range(DC):
                for dc in range(DC):
                    k.op("t", "matmul", out=ps_big[:, dc_o * 128:(dc_o + 1) * 128],
                         lhsT=gw[:, dc * VD + dc_o * 128: dc * VD + (dc_o + 1) * 128],
                         rhs=a_sb[4][:, dc * CH:(dc + 1) * CH],
                         start=(dc == 0), stop=(dc == DC - 1))
            tt = k.inc("t")
            k.wait("s", tt)
            k.op("s", "activation", out=g_sb[:, 0:512], in_=ps_big[:, 0:512], func=AF.Silu)
            k.op("s", "activation", out=g_sb[:, 512:1024], in_=ps_big[:, 512:1024], func=AF.Silu)
            tsg = k.inc("s")
            k.wait("t", tsg)

            k.wait("v", tsv)
            for j in range(NSUB):
                k.op("v", "tensor_copy", out=vtj_all[j][:, :],
                     in_=vt_sb[j * SUB:(j + 1) * SUB, :])
            for h in range(NH):
                k.op("v", "tensor_scalar", out=rb_all[:, h * CH:(h + 1) * CH],
                     in0=r_sb[:, h * CH:(h + 1) * CH],
                     scalar1=bon_c[:, h:h + 1], scalar2=None, op0=ALU.mult, op1=ALU.bypass)
            tv_prep = k.inc("v")
            k.op("s", "activation", out=fh23[:, :, 0:CH], in_=Cl3[:, :, 0:CH], func=AF.Exp)
            ts_eI = k.inc("s")
            k.wait("v", ts_eI)
            k.op("v", "tensor_tensor", out=rt_all[:, :], in0=r_sb[:, :], in1=fh2_sb[:, :],
                 op=ALU.mult)
            tv_rt = k.inc("v")
            k.wait("t", tv_rt)
            for h in range(NH):
                psO = ps_Oa if h < 2 else ps_Ob
                for q in range(2):
                    k.op("t", "matmul",
                         out=psO[:, (h % 2) * 256 + q * 128:(h % 2) * 256 + (q + 1) * 128],
                         lhsT=S_b[:, h * DV + q * 128: h * DV + (q + 1) * 128],
                         rhs=rt_all[:, h * CH:(h + 1) * CH],
                         start=(h % 2 == 0 and q == 0), stop=True,
                         skip_group_check=True)
            k.inc("t")
            prev_asm = None
            prev_slb = None
            ts_tp = None
            for j in range(NSUB):
                y0 = j * SUB
                for h in range(NH):
                    k.op("v", "tensor_scalar", out=f4[:, 0, h, :],
                         in0=Cl3[:, h, y0:y0 + SUB], scalar1=Cl3[:, h, y0:y0 + 1],
                         scalar2=None, op0=ALU.subtract, op1=ALU.bypass)
                k.op("v", "tensor_single_scalar", out=f_sc[:, 128:256], in_=f_sc[:, 0:128],
                     scalar=-60.0, op=ALU.max)
                tv_d = k.inc("v")
                k.wait("s", tv_d)
                k.op("s", "activation", out=f_sc[:, 256:384], in_=f_sc[:, 0:128], func=AF.Exp)
                k.op("s", "activation", out=f_sc[:, 384:512], in_=f_sc[:, 128:256],
                     func=AF.Exp, scale=-1.0)
                ts_e = k.inc("s")
                k.wait("v", ts_e)
                k.op("v", "tensor_tensor", out=rh3[:, :, :], in0=r3[:, :, y0:y0 + SUB],
                     in1=f4[:, 2, :, :], op=ALU.mult)
                k.op("v", "tensor_tensor", out=kh3[:, :, :], in0=k3v[:, :, y0:y0 + SUB],
                     in1=f4[:, 3, :, :], op=ALU.mult)
                k.op("v", "tensor_tensor", out=rbk3[:, :, :], in0=rb3[:, :, y0:y0 + SUB],
                     in1=k3v[:, :, y0:y0 + SUB], op=ALU.mult)
                tv_rk = k.inc("v")
                k.wait("t", tv_rk, prev_asm)
                for h in range(NH):
                    k.op("t", "matmul", out=ps_pr[0:SUB, h * SUB:(h + 1) * SUB],
                         lhsT=kh_all[:, h * SUB:(h + 1) * SUB],
                         rhs=rh_all[:, h * SUB:(h + 1) * SUB],
                         start=True, stop=True, skip_group_check=True)
                for h in range(NH):
                    k.op("t", "matmul", out=ps_pr[0:1, 128 + h * SUB:128 + (h + 1) * SUB],
                         lhsT=C.ones_col_b[:], rhs=rbk_all[:, h * SUB:(h + 1) * SUB],
                         start=True, stop=True, skip_group_check=True)
                tP = k.inc("t")
                k.wait("s", tP)
                k.op("s", "copy", out=dg_sb[:, :], in_=ps_pr[0:1, 128:256])
                ts_dg = k.inc("s")
                k.wait("t", ts_dg)
                k.op("t", "matmul", out=ps_pr[0:SUB, 256:256 + NH * SUB],
                     lhsT=C.ones_row_b[0:1, 0:SUB], rhs=dg_sb[0:1, :],
                     start=True, stop=True, skip_group_check=True)
                tdgb = k.inc("t")
                k.wait("v", tdgb)
                k.op("v", "tensor_tensor", out=P_sb[:, :], in0=ps_pr[0:SUB, 0:NH * SUB],
                     in1=C.supper4[:, :], op=ALU.mult)
                k.op("v", "tensor_tensor", out=dgd_sb[:, :], in0=ps_pr[0:SUB, 256:256 + NH * SUB],
                     in1=C.ident4[:, :], op=ALU.mult)
                k.op("v", "tensor_tensor", out=P_sb[:, :], in0=P_sb[:, :], in1=dgd_sb[:, :],
                     op=ALU.add)
                prev_asm = k.inc("v")
                k.wait("t", prev_asm, prev_slb)
                for h in range(NH):
                    psO = ps_Oa if h < 2 else ps_Ob
                    for q in range(2):
                        oc2 = (h % 2) * 256 + q * 128
                        k.op("t", "matmul", out=psO[:, oc2 + y0: oc2 + y0 + SUB],
                             lhsT=vtj_all[j][:, h * DV + q * 128: h * DV + (q + 1) * 128],
                             rhs=P_sb[:, h * SUB:(h + 1) * SUB],
                             start=False, stop=True, skip_group_check=True)
                        if j > 0:
                            k.op("t", "matmul", out=psO[:, oc2 + y0: oc2 + y0 + SUB],
                                 lhsT=sl_b[:, h * DV + q * 128: h * DV + (q + 1) * 128],
                                 rhs=rh_all[:, h * SUB:(h + 1) * SUB],
                                 start=False, stop=True, skip_group_check=True)
                tOj = k.inc("t")
                if j < NSUB - 1:
                    for h in range(NH):
                        k.op("v", "tensor_scalar", out=ecols[:, h:h + 1],
                             in0=Cl3[:, h, y0 + SUB:y0 + SUB + 1],
                             scalar1=Cl3[:, h, y0:y0 + 1],
                             scalar2=None, op0=ALU.subtract, op1=ALU.bypass)
                    tv3 = k.inc("v")
                    k.wait("s", tv3)
                    k.op("s", "activation", out=ecols[:, 4:8], in_=ecols[:, 0:4], func=AF.Exp)
                    ts3 = k.inc("s")
                    k.wait("v", ts3)
                    for h in range(NH):
                        k.op("v", "tensor_scalar", out=k3_all[:, h * SUB:(h + 1) * SUB],
                             in0=kh_all[:, h * SUB:(h + 1) * SUB],
                             scalar1=ecols[:, 4 + h:5 + h], scalar2=None,
                             op0=ALU.mult, op1=ALU.bypass)
                    tvk3 = k.inc("v")
                    k.wait("t", tvk3, ts_tp)
                    for h in range(NH):
                        k.op("t", "transpose", out=ps_tp[0:SUB, h * 128:h * 128 + 128],
                             in_=k3_all[:, h * SUB:(h + 1) * SUB], identity=C.ident_b[:])
                    ttr = k.inc("t")
                    k.wait("s", ttr)
                    k.op("s", "copy", out=k3t_sb[:, :], in_=ps_tp[0:SUB, 0:NH * 128])
                    ts_tp = k.inc("s")
                    k.wait("t", ts_tp)
                    for h in range(NH):
                        k.op("t", "matmul", out=ps_u[:, h * DV:(h + 1) * DV],
                             lhsT=k3t_sb[:, h * 128:(h + 1) * 128],
                             rhs=vtj_all[j][:, h * DV:(h + 1) * DV],
                             start=True, stop=True, skip_group_check=True)
                    tU = k.inc("t")
                    k.wait("v", tU)
                    for h in range(NH):
                        if j == 0:
                            k.op("v", "tensor_copy", out=sl_f[:, h * DV:(h + 1) * DV],
                                 in_=ps_u[:, h * DV:(h + 1) * DV])
                        else:
                            k.op("v", "scalar_tensor_tensor", out=sl_f[:, h * DV:(h + 1) * DV],
                                 in0=sl_f[:, h * DV:(h + 1) * DV], scalar=ecols[:, 4 + h:5 + h],
                                 in1=ps_u[:, h * DV:(h + 1) * DV], op0=ALU.mult, op1=ALU.add)
                    k.op("v", "tensor_copy", out=sl_b[:, :], in_=sl_f[:, :])
                    prev_slb = k.inc("v")
            tOlast = tOj
            k.wait("s", tv_rt)
            for h in range(NH):
                k.op("s", "activation", out=fh23[:, h, 0:CH], in_=Cl3[:, h, 0:CH],
                     func=AF.Exp, scale=-1.0, bias=Cl3[:, h, CH:CH + 1])
            ts_f2 = k.inc("s")
            k.wait("v", ts_f2)
            k.op("v", "tensor_tensor", out=k2_sb[:, :], in0=k_sb[:, :], in1=fh2_sb[:, :],
                 op=ALU.mult)
            tvk2 = k.inc("v")
            k.wait("t", tvk2, ts_tp)
            for h in range(NH):
                k.op("t", "transpose", out=ps_tp[0:128, h * 128:(h + 1) * 128],
                     in_=k2_sb[:, h * CH:(h + 1) * CH], identity=C.ident_b[:])
            ttr2 = k.inc("t")
            k.wait("s", ttr2)
            k.op("s", "copy", out=k2t_sb[:, :], in_=ps_tp[0:128, 0:KD])
            ts_k2t = k.inc("s")
            k.wait("t", ts_k2t, prev_slb)
            for h in range(NH):
                k.op("t", "matmul", out=ps_u[:, h * DV:(h + 1) * DV],
                     lhsT=k2t_sb[:, h * 128:(h + 1) * 128],
                     rhs=vt_sb[:, h * DV:(h + 1) * DV],
                     start=True, stop=True, skip_group_check=True)
            tU2 = k.inc("t")
            k.wait("v", tU2)
            for h in range(NH):
                k.op("v", "scalar_tensor_tensor", out=S_f[:, h * DV:(h + 1) * DV],
                     in0=S_f[:, h * DV:(h + 1) * DV],
                     scalar=fh2_sb[:, h * CH:h * CH + 1],
                     in1=ps_u[:, h * DV:(h + 1) * DV], op0=ALU.mult, op1=ALU.add)
            k.op("v", "tensor_copy", out=S_b[:, :], in_=S_f[:, :])
            tSb = k.inc("v")
            prev_tC = ts_f2
            k.wait("s", tOlast)
            k.op("s", "copy", out=o_sb[:, 0:512], in_=ps_Oa[:, 0:512])
            k.op("s", "copy", out=o_sb[:, 512:1024], in_=ps_Ob[:, 0:512])
            k.op("s", "activation", out=o2_sb[:, 0:512], in_=ps_Oa[:, 0:512], func=AF.Square)
            k.op("s", "activation", out=o2_sb[:, 512:1024], in_=ps_Ob[:, 0:512], func=AF.Square)
            ts_o = k.inc("s")
            k.wait("t", ts_o, tSb)
            for hh in range(NH):
                for q in range(2):
                    k.op("t", "matmul", out=ps_pr[0:1, hh * CH:(hh + 1) * CH],
                         lhsT=C.ones_col_b[:],
                         rhs=o_sb[:, hh * DV + q * 128: hh * DV + (q + 1) * 128],
                         start=(q == 0), stop=(q == 1), skip_group_check=True)
                for q in range(2):
                    k.op("t", "matmul", out=ps_u[0:1, hh * CH:(hh + 1) * CH],
                         lhsT=C.ones_col_b[:],
                         rhs=o2_sb[:, hh * DV + q * 128: hh * DV + (q + 1) * 128],
                         start=(q == 0), stop=(q == 1), skip_group_check=True)
            tstat = k.inc("t")
            k.wait("v", tstat)
            k.op("v", "tensor_single_scalar", out=gnrow[0:1, 0:512],
                 in_=ps_pr[0:1, 0:512], scalar=1.0 / DV, op=ALU.mult)
            k.op("v", "tensor_single_scalar", out=gnrow[0:1, 512:1024],
                 in_=ps_u[0:1, 0:512], scalar=1.0 / DV, op=ALU.mult)
            k.op("v", "tensor_tensor", out=gnrow[0:1, 1024:1536],
                 in0=gnrow[0:1, 0:512], in1=gnrow[0:1, 0:512], op=ALU.mult)
            k.op("v", "tensor_tensor", out=gnrow[0:1, 1024:1536],
                 in0=gnrow[0:1, 512:1024], in1=gnrow[0:1, 1024:1536], op=ALU.subtract)
            tv_var = k.inc("v")
            k.wait("s", tv_var)
            k.op("s", "activation", out=gnrow[0:1, 1536:2048], in_=gnrow[0:1, 1024:1536],
                 func=AF.Ln, bias=C.eps_ln[0:1, 0:1], scale=1.0)
            k.op("s", "activation", out=gnrow[0:1, 1536:2048], in_=gnrow[0:1, 1536:2048],
                 func=AF.Exp, scale=-0.5)
            ts_gn = k.inc("s")
            k.wait("v", ts_gn)
            k.op("v", "tensor_tensor", out=gnrow[0:1, 1024:1536],
                 in0=gnrow[0:1, 0:512], in1=gnrow[0:1, 1536:2048], op=ALU.mult)
            k.op("v", "tensor_single_scalar", out=gnrow[0:1, 1024:1536],
                 in_=gnrow[0:1, 1024:1536], scalar=-1.0, op=ALU.mult)
            tv_nb = k.inc("v")
            k.wait("t", tv_nb)
            k.op("t", "matmul", out=ps_big[:, 0:512], lhsT=C.ones_row_f[:],
                 rhs=gnrow[0:1, 1536:2048], start=True, stop=True, skip_group_check=True)
            k.op("t", "matmul", out=ps_big[:, 512:1024], lhsT=C.ones_row_f[:],
                 rhs=gnrow[0:1, 1024:1536], start=True, stop=True, skip_group_check=True)
            tbc = k.inc("t")
            k.wait("v", tbc, tsg)
            for h in range(NH):
                for q in range(2):
                    dcg = h * 2 + q
                    k.op("v", "tensor_tensor", out=on_sb[:, dcg * 128:(dcg + 1) * 128],
                         in0=o_sb[:, h * DV + q * 128: h * DV + (q + 1) * 128],
                         in1=ps_big[:, h * 128:(h + 1) * 128], op=ALU.mult)
            for h in range(NH):
                for q in range(2):
                    dcg = h * 2 + q
                    k.op("v", "tensor_tensor", out=on_sb[:, dcg * 128:(dcg + 1) * 128],
                         in0=on_sb[:, dcg * 128:(dcg + 1) * 128],
                         in1=ps_big[:, 512 + h * 128: 512 + (h + 1) * 128], op=ALU.add)
            for dcg in range(DC):
                k.op("v", "tensor_scalar", out=on_sb[:, dcg * 128:(dcg + 1) * 128],
                     in0=on_sb[:, dcg * 128:(dcg + 1) * 128],
                     scalar1=gnw_c[:, dcg:dcg + 1], scalar2=gnb_c[:, dcg:dcg + 1],
                     op0=ALU.mult, op1=ALU.add)
            for dcg in range(DC):
                k.op("v", "tensor_tensor", out=on_sb[:, dcg * 128:(dcg + 1) * 128],
                     in0=on_sb[:, dcg * 128:(dcg + 1) * 128],
                     in1=g_sb[:, dcg * 128:(dcg + 1) * 128], op=ALU.mult)
            ton = k.inc("v")

            k.wait("t", ton)
            for half in range(2):
                for dc in range(DC):
                    k.op("t", "matmul", out=ps_big[:, half * 512:(half + 1) * 512],
                         lhsT=on_sb[:, dc * 128:(dc + 1) * 128],
                         rhs=ow[:, dc * D + half * 512: dc * D + (half + 1) * 512],
                         start=(dc == 0), stop=(dc == DC - 1))
            tsp = k.inc("t")
            ddx = k.dma("q", out=xt_f[:, :], in_=x1_r[:, ci, :], wait=(k.tok("v"),))
            ddy = k.dma("q", out=ymt[:, :], in_=ym_r[:, ci, :])
            k.wait("s", ddy)
            k.op("s", "activation", out=g1t[:, :], in_=ymt[:, :], func=AF.Silu)
            ts9 = k.inc("s")
            k.wait("v", tsp, ts9, ddx)
            k.op("v", "tensor_tensor", out=g1t[:, :], in0=ps_big[:, 0:D], in1=g1t[:, :], op=ALU.mult)
            tv = k.inc("v")
            k.wait("v", tv)
            k.op("v", "tensor_tensor", out=xt_f[:, :], in0=xt_f[:, :], in1=g1t[:, :], op=ALU.add)
            tv = k.inc("v")
            k.dma("q", out=y_r[:, ci, :], in_=xt_f[:, :], wait=(tv,))
            k.wait("v", ("d", "q", k.dcnt["q"]))
            k.wait("t", k.tok("v"))
        k.fence_all()


def _f32c(a):
    return np.ascontiguousarray(np.asarray(a, dtype=np.float32))


L1_PHASES = "lgms"
MOE_FENCE_H = False
MOE_FENCE_Y = False
MOE_FENCE_TSB = False
MOE_FENCE_E = False
MOE_EXPERTS = NE


def emit_l1(k, W, x1_in, ymoe_scr, y_out, dbg=None):
    nc, es = k.nc, k.es
    S = lambda *a, **kw: es.enter_context(nc.sbuf_tensor(*a, **kw))
    xn1_fm = S("l1_xn_fm", [128, DC * TS], BF16)
    dw_sb = S("l1_dw", [128, NT * NE], F32)
    l1es = contextlib.ExitStack()
    xn_lo_fm = l1es.enter_context(nc.sbuf_tensor("l1_xn_lo", [128, DC * T], BF16))
    if "l" in L1_PHASES:
        emit_l1_ln(k, W, x1_in, xn1_fm, xn_lo_fm, dbg=dbg)
    if dbg and "xn1_fm" in dbg:
        k.dma("q", out=dbg["xn1_fm"], in_=xn1_fm.ap())
        k.fence_all()
    if "g" in L1_PHASES:
        emit_l1_gate(k, W, xn1_fm, xn_lo_fm, dw_sb, dbg=dbg)
    l1es.close()
    if dbg and "dw" in dbg:
        k.dma("q", out=dbg["dw"], in_=dw_sb.ap())
        k.fence_all()
    if "m" in L1_PHASES:
        emit_l1_moe(k, W, xn1_fm, dw_sb, ymoe_scr, dbg=dbg)
    if "s" in L1_PHASES:
        emit_l1_scan(k, W, x1_in, ymoe_scr, y_out, xn1_fm, dbg=dbg)
    else:
        for i in range(NT):
            k.dma("g", out=y_out.rearrange("(n p) d -> p n d", p=128)[:, i, :],
                  in_=xn1_fm[:, 0:D])
        k.fence_all()


WEIGHT_SPECS_L1 = {
    "gate_wT": [D, NE],
    "moe_w1T": [NE, D, NI],
    "moe_w2T": [NE, NI, D],
    "moe_w3T": [NE, D, NI],
    "x_proj_mu_c": [128, DC],
    "xl_w1T": [D, RLO],
    "xl_w2T": [32, R5],
    "xlb2_cols": [128, 2],
    "x_out_wTn": [33, 5 * D],
    "r_wT": [D, KD],
    "k_wT": [D, KD],
    "v_wT": [D, VD],
    "g_wT": [D, VD],
    "o_wT": [VD, D],
    "w_w1T": [D, GD],
    "w_w2T": [GD, KD],
    "w_b_c": [128, NH],
    "bonus_c": [128, NH],
    "gn_w_c": [128, DC],
    "gn_b_c": [128, DC],
}


def host_prep_l1(inputs):
    o = {}
    g = lambda nm: np.asarray(inputs[nm], dtype=np.float32)
    o["gate_wT"] = _f32c(g("L1_ffn_gate_w").T)
    o["moe_w1T"] = _f32c(g("L1_ffn_w1").transpose(0, 2, 1))
    o["moe_w2T"] = _f32c(g("L1_ffn_w2").transpose(0, 2, 1))
    o["moe_w3T"] = _f32c(g("L1_ffn_w3").transpose(0, 2, 1))
    o["x_proj_mu_c"] = _f32c(g("L1_attn_x_proj_mu").reshape(DC, 128).T)
    o["xl_w1T"] = _f32c(g("L1_attn_xl_w1").T)
    o["xl_w2T"] = _f32c(g("L1_attn_xl_w2").T)
    b2 = g("L1_attn_xl_b2")
    xc = np.zeros((128, 2), np.float32)
    xc[0:96, 0] = b2[0:96]
    xc[0:64, 1] = b2[96:160]
    o["xlb2_cols"] = xc
    xo = g("L1_attn_x_out_w").reshape(D, 5, 32)
    xow = np.concatenate([xo[:, n, :].T for n in range(5)], axis=1)
    xb = g("L1_attn_x_bias").reshape(1, 5 * D)
    o["x_out_wTn"] = _f32c(np.concatenate([xow, xb], axis=0))
    o["r_wT"] = _f32c(g("L1_attn_r_w").T)
    o["k_wT"] = _f32c(g("L1_attn_k_w").T)
    o["v_wT"] = _f32c(g("L1_attn_v_w").T)
    o["g_wT"] = _f32c(g("L1_attn_g_w").T)
    o["o_wT"] = _f32c(g("L1_attn_o_w").T)
    o["w_w1T"] = _f32c(g("L1_attn_w_w1").T)
    o["w_w2T"] = _f32c(g("L1_attn_w_w2").T)
    o["w_b_c"] = _f32c(g("L1_attn_w_b").reshape(NH, 128).T)
    o["bonus_c"] = _f32c(g("L1_attn_bonus").T)
    o["gn_w_c"] = _f32c(g("L1_attn_gn_weight").reshape(DC, 128).T)
    o["gn_b_c"] = _f32c(g("L1_attn_gn_bias").reshape(DC, 128).T)
    return o


def build_l1_program(dbg_names=()):
    nc = bass.Bass("TRN2", target_bir_lowering=False)
    x1 = nc.dram_tensor("x1_in", [T, D], F32, kind="ExternalInput").ap()
    y = nc.dram_tensor("y", [T, D], F32, kind="ExternalOutput").ap()
    W = {nm: nc.dram_tensor(nm, shape, F32, kind="ExternalInput").ap()
         for nm, shape in WEIGHT_SPECS_L1.items()}
    ymoe = nc.dram_tensor("ymoe", [T, D], F32,
                          kind="ExternalOutput" if "ymoe" in dbg_names else "Internal").ap()
    dbg = {}
    specs = {"xn1_fm": ([128, DC * TS], BF16), "dw": ([128, NT * NE], F32)}
    for nm in set(dbg_names) & set(specs):
        shape, dt = specs[nm]
        dbg[nm] = nc.dram_tensor("dbg_" + nm, shape, dt, kind="ExternalOutput").ap()
    if "ymoe" in dbg_names:
        dbg["ymoe"] = True
    es = contextlib.ExitStack()
    with es:
        k = K(nc, es)
        block = es.enter_context(nc.Block())

        @block.sync
        def _(sync):
            alloc_consts(k)
            emit_l1(k, W, x1, ymoe, y, dbg=dbg if dbg else None)
            k.fence_all()
    return nc


def build_full_program():
    nc = bass.Bass("TRN2", target_bir_lowering=False)
    x_in = nc.dram_tensor("x_in", [T, D], F32, kind="ExternalInput").ap()
    y = nc.dram_tensor("y", [T, D], F32, kind="ExternalOutput").ap()
    W = {nm: nc.dram_tensor(nm, shape, F32, kind="ExternalInput").ap()
         for nm, shape in WEIGHT_SPECS_L0.items()}
    for nm, shape in WEIGHT_SPECS_L1.items():
        W[nm] = nc.dram_tensor(nm, shape, F32, kind="ExternalInput").ap()
    W["w1t_scr"] = nc.dram_tensor("w1t_scr", [D, FF], BF16, kind="Internal").ap()
    W["w2t_scr"] = nc.dram_tensor("w2t_scr", [FF, D], BF16, kind="Internal").ap()
    x1 = nc.dram_tensor("x1_scr", [T, D], F32, kind="Internal").ap()
    ymoe = nc.dram_tensor("ymoe_scr", [T, D], F32, kind="Internal").ap()
    es = contextlib.ExitStack()
    with es:
        k = K(nc, es)
        block = es.enter_context(nc.Block())

        @block.sync
        def _(sync):
            alloc_consts(k)
            emit_l0(k, W, x_in, x1, dbg=None)
            emit_l1(k, W, x1, ymoe, y, dbg=None)
            k.fence_all()
    return nc



def host_prep_l0(inputs):
    o = {}
    o["w_qkvT"] = _f32c(np.asarray(inputs["L0_attn_w_qkv"]).T)
    b = np.asarray(inputs["L0_attn_b_qkv"], dtype=np.float32)
    o["bqkv_qk_c"] = _f32c(b[:2048].reshape(16, 128).T)
    o["bqkv_v_row"] = _f32c(b[2048:].reshape(1, D))
    o["w_outT"] = _f32c(np.asarray(inputs["L0_attn_w_out"]).T)
    o["b_out_row"] = _f32c(np.asarray(inputs["L0_attn_b_out"]).reshape(1, D))
    o["w1T"] = _f32c(np.asarray(inputs["L0_ffn_W1"]).T)
    o["b1_c"] = _f32c(np.asarray(inputs["L0_ffn_b1"]).reshape(32, 128).T)
    o["w2T"] = _f32c(np.asarray(inputs["L0_ffn_W2"]).T)
    o["b2_row"] = _f32c(np.asarray(inputs["L0_ffn_b2"]).reshape(1, D))
    return o


WEIGHT_SPECS_L0 = {
    "w_qkvT": [D, 3 * D],
    "bqkv_qk_c": [128, 16],
    "bqkv_v_row": [1, D],
    "w_outT": [D, D],
    "b_out_row": [1, D],
    "w1T": [D, FF],
    "b1_c": [128, 32],
    "w2T": [FF, D],
    "b2_row": [1, D],
}


def build_l0_program(dbg_names=()):
    nc = bass.Bass("TRN2", target_bir_lowering=False)
    x_in = nc.dram_tensor("x_in", [T, D], F32, kind="ExternalInput").ap()
    x1 = nc.dram_tensor("x1", [T, D], F32, kind="ExternalOutput").ap()
    W = {nm: nc.dram_tensor(nm, shape, F32, kind="ExternalInput").ap()
         for nm, shape in WEIGHT_SPECS_L0.items()}
    W["w1t_scr"] = nc.dram_tensor("w1t_scr", [D, FF], BF16, kind="Internal").ap()
    W["w2t_scr"] = nc.dram_tensor("w2t_scr", [FF, D], BF16, kind="Internal").ap()
    dbg = {}
    specs = {"xn0_fm": ([128, DC * T], BF16), "q_ext": ([128, DC * (T + 16)], BF16),
             "k_fm": ([128, DC * T], BF16), "v_tm": ([128, NT * H0 * (HD0 + 1)], BF16),
             "s_tm": ([128, NT * D], BF16)}
    want = set(dbg_names)
    if "qkv" in want:
        want |= {"q_ext", "k_fm", "v_tm"}
    for nm in want & set(specs):
        shape, dt = specs[nm]
        dbg[nm] = nc.dram_tensor("dbg_" + nm, shape, dt, kind="ExternalOutput").ap()
    if "qkv" in dbg_names:
        dbg["qkv"] = True
    es = contextlib.ExitStack()
    with es:
        k = K(nc, es)
        block = es.enter_context(nc.Block())

        @block.sync
        def _(sync):
            alloc_consts(k)
            emit_l0(k, W, x_in, x1, dbg=dbg if dbg else None)
            k.fence_all()
    return nc


_PROG_CACHE = {}


def kernel(**inputs):
    x_full = np.ascontiguousarray(np.asarray(inputs["L0_attn_input_embeddings"], dtype=np.float32))
    W = host_prep_l0(inputs)
    W.update(host_prep_l1(inputs))
    if "full" not in _PROG_CACHE:
        _PROG_CACHE["full"] = build_full_program()
    nc = _PROG_CACHE["full"]
    in_maps = []
    for core in range(8):
        m = dict(W)
        m["x_in"] = np.ascontiguousarray(x_full[core])
        in_maps.append(m)
    res = run_bass_kernel_spmd(nc, in_maps, core_ids=list(range(8)))
    out = np.stack([np.asarray(res.results[i]["y"], dtype=np.float32) for i in range(8)], axis=0)
    return out


# version 06
# speedup vs baseline: 1.1214x; 1.1214x over previous
import contextlib
import numpy as np
import ml_dtypes
import concourse.bass as bass
import concourse.mybir as mybir
from concourse.bass_utils import run_bass_kernel_spmd

F32 = mybir.dt.float32
BF16 = mybir.dt.bfloat16
I32 = mybir.dt.int32
AF = mybir.ActivationFunctionType
ALU = mybir.AluOpType
AX = mybir.AxisListType

T = 2048
D = 1024
DC = 8
NT = 16
H0, HD0 = 16, 64
FF = 4096
WIN = 144
NH, DK, DV, KD, VD = 4, 128, 256, 512, 1024
RLO, R5, GD = 32, 160, 64
NE, NI = 8, 2048
EPS_LN = 1e-5
EPS_GN = 1e-5
JIT2 = 0.02
NEG = -1e30

ENGK = ("t", "v", "s", "g", "q")
ENGNAME = dict(t="tensor", v="vector", s="scalar", g="gpsimd", q="sync")


class K:

    def __init__(self, nc, es):
        self.nc = nc
        self.es = es
        self.e = {k: getattr(nc, ENGNAME[k]) for k in ENGK}
        self.sem = {k: es.enter_context(nc.semaphore("S_" + k)) for k in ENGK}
        self.dsem = {k: es.enter_context(nc.semaphore("D_" + k)) for k in ("q", "s", "g")}
        self.cnt = {k: 0 for k in ENGK}
        self.dcnt = {k: 0 for k in ("q", "s", "g")}
        self.waited = {}
        self.last = {k: None for k in ENGK}
        self.bank_guard = {}

    def op(self, ek, name, *a, **kw):
        instr = getattr(self.e[ek], name)(*a, **kw)
        self.last[ek] = instr
        return instr

    def inc(self, ek):
        if self.last[ek] is not None:
            self.cnt[ek] += 1
            self.last[ek].then_inc(self.sem[ek], 1)
            self.last[ek] = None
        return ("c", ek, self.cnt[ek])

    def tok(self, ek):
        return ("c", ek, self.cnt[ek])

    def dma(self, qk, out, in_, wait=(), **kw):
        for tk in wait:
            self.wait(qk, tk)
        i = self.e[qk].dma_start(out=out, in_=in_, **kw)
        self.dcnt[qk] += 16
        i.then_inc(self.dsem[qk], 16)
        return ("d", qk, self.dcnt[qk])

    def wait(self, ek, *toks):
        for tk in toks:
            if tk is None:
                continue
            kind, src, val = tk
            if val <= 0:
                continue
            key = (ek, kind, src)
            if self.waited.get(key, 0) >= val:
                continue
            sem = self.sem[src] if kind == "c" else self.dsem[src]
            self.e[ek].wait_ge(sem, val)
            self.waited[key] = val

    def fence_all(self, waiters=ENGK):
        toks = [self.inc(src) for src in ENGK]
        toks += [("d", qk, self.dcnt[qk]) for qk in ("q", "s", "g")]
        for ek in waiters:
            self.wait(ek, *toks)


class Consts:
    pass


def alloc_consts(k):
    nc, es = k.nc, k.es
    S = lambda *a, **kw: es.enter_context(nc.sbuf_tensor(*a, **kw))
    C = Consts()
    C.iota = S("c_iota", [128, 256], I32)
    C.ident_f = S("c_ident_f", [128, 128], F32)
    C.ident_b = S("c_ident_b", [128, 128], BF16)
    C.band_b = S("c_band_b", [128, WIN], BF16)
    C.supper_b = S("c_supper_b", [128, 128], BF16)
    C.supper4 = S("c_supper4", [32, 128], BF16)
    C.ident4 = S("c_ident4", [32, 128], BF16)
    C.ones_col_f = S("c_ones_col_f", [128, 1], F32)
    C.ones_col_b = S("c_ones_col_b", [128, 1], BF16)
    C.ones_row_f = S("c_ones_row_f", [1, 128], F32)
    C.ones_row_b = S("c_ones_row_b", [1, 128], BF16)
    C.scr_f = S("c_scr_f", [128, 2 * WIN], F32)
    C.eps_ln = S("c_eps_ln", [128, 1], F32)
    C.m12 = S("c_m12", [128, 1], F32)
    C.zero_f = S("c_zero_f", [128, 1], F32)
    k.C = C
    g, v = "g", "v"
    k.op(g, "iota", out=C.iota[:], pattern=[[1, 256]], base=0, channel_multiplier=-1)
    ti = k.inc(g)
    k.wait(v, ti)
    k.op(v, "tensor_single_scalar", out=C.ident_f[:], in_=C.iota[:, 0:128], scalar=0, op=ALU.is_equal)
    k.op(v, "tensor_single_scalar", out=C.ident_b[:], in_=C.iota[:, 0:128], scalar=0, op=ALU.is_equal)
    for b in range(4):
        k.op(v, "tensor_single_scalar", out=C.supper4[0:32, b * 32:(b + 1) * 32],
             in_=C.iota[0:32, b * 32:(b + 1) * 32], scalar=b * 32, op=ALU.is_gt)
        k.op(v, "tensor_single_scalar", out=C.ident4[0:32, b * 32:(b + 1) * 32],
             in_=C.iota[0:32, b * 32:(b + 1) * 32], scalar=b * 32, op=ALU.is_equal)
    k.op(v, "tensor_single_scalar", out=C.scr_f[:, 0:WIN], in_=C.iota[:, 0:WIN], scalar=0, op=ALU.is_ge)
    k.op(v, "tensor_single_scalar", out=C.scr_f[:, WIN:2 * WIN], in_=C.iota[:, 0:WIN], scalar=16, op=ALU.is_le)
    tv = k.inc(v)
    k.wait(v, tv)
    k.op(v, "tensor_tensor", out=C.band_b[:], in0=C.scr_f[:, 0:WIN], in1=C.scr_f[:, WIN:2 * WIN], op=ALU.mult)
    k.op(v, "tensor_single_scalar", out=C.supper_b[:], in_=C.iota[:, 0:128], scalar=0, op=ALU.is_gt)
    k.op(v, "memset", C.ones_col_f[:], 1.0)
    k.op(v, "memset", C.ones_col_b[:], 1.0)
    k.op(v, "memset", C.ones_row_f[:], 1.0)
    k.op(v, "memset", C.ones_row_b[:], 1.0)
    k.op(v, "memset", C.eps_ln[:], EPS_LN)
    k.op(v, "memset", C.m12[:], -12.0)
    k.op(v, "memset", C.zero_f[:], 0.0)
    k.inc(v)
    k.fence_all()


def emit_bcast_row(k, dst, src_row, psum, n):
    C = k.C
    ts = None
    for c0 in range(0, n, 512):
        c1 = min(c0 + 512, n)
        k.op("t", "matmul", out=psum[:, 0:c1 - c0], lhsT=C.ones_row_f[:],
             rhs=src_row[0:1, c0:c1], start=True, stop=True)
        tt = k.inc("t")
        k.wait("s", tt)
        k.op("s", "copy", out=dst[:, c0:c1], in_=psum[:, 0:c1 - c0])
        ts = k.inc("s")
        k.wait("t", ts)
    return ts


def emit_ln_tile(k, x_tile, xn_out, stats, eps=EPS_LN, extra_out=None):
    v, s = "v", "s"
    k.op(v, "bn_stats", out=stats[:, 0:6], in_=x_tile[:, 0:512])
    k.op(v, "bn_stats", out=stats[:, 6:12], in_=x_tile[:, 512:1024])
    t1 = k.inc(v)
    k.wait(v, t1)
    k.op(v, "bn_aggr", out=stats[:, 12:14], in_=stats[:, 0:12])
    t2 = k.inc(v)
    k.wait(v, t2)
    k.op(v, "tensor_single_scalar", out=stats[:, 14:15], in_=stats[:, 12:13], scalar=-1.0, op=ALU.mult)
    tneg = k.inc(v)
    k.wait(s, t2)
    k.op(s, "activation", out=stats[:, 15:16], in_=stats[:, 13:14], func=AF.Ln, bias=k.C.eps_ln[:], scale=1.0)
    t3 = k.inc(s)
    k.wait(s, t3)
    k.op(s, "activation", out=stats[:, 16:17], in_=stats[:, 15:16], func=AF.Exp, scale=-0.5)
    t4 = k.inc(s)
    k.wait(v, t4, tneg)
    k.op(v, "tensor_scalar", out=xn_out, in0=x_tile[:, 0:D], scalar1=stats[:, 14:15],
         scalar2=stats[:, 16:17], op0=ALU.add, op1=ALU.mult)
    tk = k.inc(v)
    if extra_out is not None:
        k.op(v, "tensor_scalar", out=extra_out, in0=x_tile[:, 0:D], scalar1=stats[:, 14:15],
             scalar2=stats[:, 16:17], op0=ALU.add, op1=ALU.mult)
        tk = k.inc(v)
    return tk


def emit_transpose_tile(k, src_tile, dst_fn, psA, psB, dep=None, evac_eng="s"):
    C = k.C
    tok = None
    for dc in range(DC):
        ps = psA if dc % 2 == 0 else psB
        k.wait("t", dep, k.bank_guard.get(id(ps)))
        k.op("t", "transpose", out=ps[:], in_=src_tile[:, dc * 128:(dc + 1) * 128], identity=C.ident_b[:])
        tt = k.inc("t")
        k.wait(evac_eng, tt)
        k.op(evac_eng, "copy" if evac_eng == "s" else "tensor_copy", out=dst_fn(dc), in_=ps[:])
        tok = k.inc(evac_eng)
        k.bank_guard[id(ps)] = tok
    return tok


def emit_l0(k, W, x_in, x1_out, dbg=None):
    nc = k.nc
    C = k.C
    l0es = contextlib.ExitStack()
    es = l0es
    S = lambda *a, **kw: es.enter_context(nc.sbuf_tensor(*a, **kw))

    x_r = x_in.rearrange("(n p) d -> p n d", p=128)
    x1_r = x1_out.rearrange("(n p) d -> p n d", p=128)

    xn_fm = S("l0_xn_fm", [128, DC * T], BF16)
    s_tm = S("l0_s_tm", [128, NT * D], BF16)
    brow = S("l0_brow", [1, D], F32)

    with contextlib.ExitStack() as pes:
        PS = lambda *a, **kw: pes.enter_context(nc.psum_tensor(*a, **kw))
        SS = lambda *a, **kw: pes.enter_context(nc.sbuf_tensor(*a, **kw))
        ps_a = PS("pa_ps", [128, 512], F32)
        wbig = [SS(f"pa_wbig{i}", [128, FF], F32) for i in range(2)]
        wint = [SS(f"pa_wint{i}", [128, FF], I32) for i in range(2)]
        wtb = [SS(f"pa_wtb{i}", [128, FF], BF16) for i in range(2)]
        acc = SS("pa_acc", [128, 2], F32)
        tots = SS("pa_tots", [1, 2], F32)
        st = SS("pa_st", [128, 2], F32)
        rs1 = SS("pa_rs1", [128, 1], F32)
        rs2 = SS("pa_rs2", [128, 1], F32)

        vdone = {}
        ric = [0]
        for wi, (wnm, rows, rsdst) in enumerate([("w1T", D, rs1), ("w2T", FF, rs2)]):
            nr = rows // 128
            per = FF if wnm == "w1T" else D
            k.op("v", "memset", acc[:, wi:wi + 1], 0.0)
            k.inc("v")
            for r in range(nr):
                ri0 = ric[0]
                ric[0] += 1
                p = ri0 % 2
                dd = k.dma("q" if p == 0 else "g", out=wbig[p][:, 0:per],
                           in_=W[wnm].rearrange("(c p) f -> p c f", p=128)[:, r, :],
                           wait=(vdone.get(ri0 - 2),))
                k.wait("v", dd)
                k.op("v", "reduce_sum", out=st[:, p:p + 1], in_=wbig[p][:, 0:per], axis=AX.X,
                     apply_absolute_value=True)
                tred = k.inc("v")
                k.wait("v", tred)
                k.op("v", "tensor_tensor", out=acc[:, wi:wi + 1], in0=acc[:, wi:wi + 1],
                     in1=st[:, p:p + 1], op=ALU.add)
                vdone[ri0] = k.inc("v")
            tv = k.inc("v")
            k.wait("t", tv)
            k.op("t", "matmul", out=ps_a[0:1, 0:1], lhsT=acc[:, wi:wi + 1], rhs=C.ones_col_f[:],
                 start=True, stop=True)
            tt = k.inc("t")
            k.wait("v", tt)
            k.op("v", "reciprocal", out=tots[0:1, wi:wi + 1], in_=ps_a[0:1, 0:1])
            tv = k.inc("v")
            k.wait("v", tv)
            k.op("v", "tensor_single_scalar", out=tots[0:1, wi:wi + 1], in_=tots[0:1, wi:wi + 1],
                 scalar=float(FF * D), op=ALU.mult)
            tv = k.inc("v")
            k.wait("t", tv)
            k.op("t", "matmul", out=ps_a[:, 0:1], lhsT=C.ones_row_f[:], rhs=tots[0:1, wi:wi + 1],
                 start=True, stop=True)
            tt = k.inc("t")
            k.wait("s", tt)
            k.op("s", "copy", out=rsdst[:], in_=ps_a[:, 0:1])
            ts = k.inc("s")
            k.wait("t", ts)

        tv_last = k.tok("v")
        vguard = {-2: tv_last, -1: tv_last}
        dguard = {}
        ri = 0
        for wnm, rows, rsdst, scr in [("w1T", D, rs1, "w1t_scr"), ("w2T", FF, rs2, "w2t_scr")]:
            nr = rows // 128
            per = FF if wnm == "w1T" else D
            for r in range(nr):
                p = ri % 2
                qk = "q" if p == 0 else "g"
                dd = k.dma(qk, out=wbig[p][:, 0:per],
                           in_=W[wnm].rearrange("(c p) f -> p c f", p=128)[:, r, :],
                           wait=(vguard.get(ri - 2),))
                k.wait("v", dd, dguard.get(ri - 2))
                k.op("v", "tensor_scalar", out=wint[p][:, 0:per], in0=wbig[p][:, 0:per],
                     scalar1=rsdst[:], scalar2=None, op0=ALU.mult, op1=ALU.bypass)
                k.op("v", "tensor_scalar", out=wtb[p][:, 0:per], in0=wint[p][:, 0:per],
                     scalar1=1, scalar2=-1, op0=ALU.min, op1=ALU.max)
                tv = k.inc("v")
                vguard[ri] = tv
                dguard[ri] = k.dma(qk, out=W[scr].rearrange("(c p) f -> p c f", p=128)[:, r, :],
                                   in_=wtb[p][:, 0:per], wait=(tv,))
                ri += 1
        k.fence_all()

    with contextlib.ExitStack() as pes:
        PS = lambda *a, **kw: pes.enter_context(nc.psum_tensor(*a, **kw))
        SS = lambda *a, **kw: pes.enter_context(nc.sbuf_tensor(*a, **kw))
        ps_ta = PS("pb_ps_ta", [128, 128], BF16)
        ps_tb = PS("pb_ps_tb", [128, 128], BF16)
        xt = [SS(f"pb_xt{i}", [128, D], F32) for i in range(2)]
        xnt = [SS(f"pb_xnt{i}", [128, D], BF16) for i in range(2)]
        stats = SS("pb_stats", [128, 34], F32)
        prev = [None, None]
        for i in range(NT):
            p = i % 2
            dd = k.dma("q", out=xt[p][:, :], in_=x_r[:, i, :], wait=(prev[p],))
            k.wait("v", dd)
            tln = emit_ln_tile(k, xt[p], xnt[p][:, :], stats[:, p * 17:p * 17 + 17])
            tlast = emit_transpose_tile(
                k, xnt[p], lambda dc, i=i: xn_fm[:, dc * T + i * 128: dc * T + (i + 1) * 128],
                ps_ta, ps_tb, dep=tln)
            prev[p] = tlast
        k.fence_all()

    if dbg and "xn0_fm" in dbg:
        k.dma("q", out=dbg["xn0_fm"], in_=xn_fm.ap())
        k.fence_all()

    attn_es = contextlib.ExitStack()
    with contextlib.ExitStack() as pes:
        PS = lambda *a, **kw: pes.enter_context(nc.psum_tensor(*a, **kw))
        SS = lambda *a, **kw: pes.enter_context(nc.sbuf_tensor(*a, **kw))
        q_ext = attn_es.enter_context(nc.sbuf_tensor("l0_q_ext", [128, DC * (T + 16)], BF16))
        k_fm = attn_es.enter_context(nc.sbuf_tensor("l0_k_fm", [128, DC * T], BF16))
        v_tm = attn_es.enter_context(nc.sbuf_tensor("l0_v_tm", [128, NT * H0 * (HD0 + 1)], BF16))
        k.q_ext, k.k_fm, k.v_tm = q_ext, k_fm, v_tm
        ps_a = PS("pc_ps_a", [128, 512], F32)
        ps_b = PS("pc_ps_b", [128, 512], F32)
        wqs = [SS(f"pc_wqs{i}", [128, DC * 128], BF16) for i in range(2)]
        wvs = [SS(f"pc_wvs{i}", [128, DC * 512], BF16) for i in range(2)]
        bqkv_c = SS("pc_bqkv_c", [128, 16], F32)
        bv_bc = SS("pc_bv_bc", [128, D], F32)

        k.dma("q", out=bqkv_c[:, :], in_=W["bqkv_qk_c"])
        dv_ = k.dma("q", out=brow[:, :], in_=W["bqkv_v_row"])
        k.wait("t", dv_)
        emit_bcast_row(k, bv_bc, brow[0:1, 0:D], ps_a, D)

        qv = q_ext.ap().rearrange("p (c n) -> p c n", c=DC)
        k.op("v", "memset", qv[:, :, 0:8], 0.0)
        k.op("v", "memset", qv[:, :, 8 + T:16 + T], 0.0)
        vv = v_tm.ap().rearrange("p (n h c) -> p n h c", n=NT, h=H0)
        k.op("v", "memset", vv[:, :, :, HD0:HD0 + 1], 1.0)
        k.inc("v")

        wq_r = W["w_qkvT"].rearrange("(c p) f -> p c f", p=128)
        oc_done = {}
        for oc in range(16):
            wb = wqs[oc % 2]
            dd = k.dma("g", out=wb.ap().rearrange("p (c f) -> p c f", c=DC),
                       in_=wq_r[:, :, oc * 128:(oc + 1) * 128], wait=(oc_done.get(oc - 2),))
            k.wait("t", dd)
            for tsb in range(4):
                ps = ps_a if tsb % 2 == 0 else ps_b
                for dc in range(DC):
                    k.op("t", "matmul", out=ps[:, 0:512],
                         lhsT=wb[:, dc * 128:(dc + 1) * 128],
                         rhs=xn_fm[:, dc * T + tsb * 512: dc * T + (tsb + 1) * 512],
                         start=(dc == 0), stop=(dc == DC - 1))
                tt = k.inc("t")
                k.wait("s", tt)
                if oc < 8:
                    dst = q_ext[:, oc * (T + 16) + 8 + tsb * 512: oc * (T + 16) + 8 + (tsb + 1) * 512]
                else:
                    dst = k_fm[:, (oc - 8) * T + tsb * 512: (oc - 8) * T + (tsb + 1) * 512]
                k.op("s", "activation", out=dst, in_=ps[:, 0:512], func=AF.Identity,
                     bias=bqkv_c[:, oc:oc + 1], scale=1.0)
                ts = k.inc("s")
                k.wait("t", ts)
            oc_done[oc] = k.tok("s")
        half_done = {}
        for half in range(2):
            wb = wvs[half % 2]
            dd = k.dma("g", out=wb.ap().rearrange("p (c f) -> p c f", c=DC),
                       in_=wq_r[:, :, 2048 + half * 512: 2048 + (half + 1) * 512])
            k.wait("t", dd)
            for tt_i in range(NT):
                ps = ps_a if tt_i % 2 == 0 else ps_b
                for dc in range(DC):
                    k.op("t", "matmul", out=ps[:, 0:512],
                         lhsT=xn_fm[:, dc * T + tt_i * 128: dc * T + (tt_i + 1) * 128],
                         rhs=wb[:, dc * 512:(dc + 1) * 512],
                         start=(dc == 0), stop=(dc == DC - 1))
                tmm = k.inc("t")
                k.wait("v", tmm)
                k.op("v", "tensor_tensor", out=vv[:, tt_i, half * 8:(half + 1) * 8, 0:HD0],
                     in0=ps.ap()[:, 0:512].rearrange("p (h c) -> p h c", h=8),
                     in1=bv_bc.ap()[:, half * 512:(half + 1) * 512].rearrange("p (h c) -> p h c", h=8),
                     op=ALU.add)
                tv = k.inc("v")
                k.wait("t", tv)
        k.fence_all()

    if dbg and "qkv" in dbg:
        k.dma("q", out=dbg["q_ext"], in_=k.q_ext.ap())
        k.dma("q", out=dbg["k_fm"], in_=k.k_fm.ap())
        k.dma("q", out=dbg["v_tm"], in_=k.v_tm.ap())
        k.fence_all()

    with contextlib.ExitStack() as pes:
        PS = lambda *a, **kw: pes.enter_context(nc.psum_tensor(*a, **kw))
        SS = lambda *a, **kw: pes.enter_context(nc.sbuf_tensor(*a, **kw))
        q_ext, k_fm, v_tm = k.q_ext, k.k_fm, k.v_tm
        woT_b = SS("pd_woT_b", [128, DC * D], BF16)
        bout_bc = SS("pd_bout_bc", [128, D], F32)
        E_sb = SS("pd_E", [128, 3 * H0 * WIN], BF16)
        ctx_fm = SS("pd_ctx", [128, DC * 128], BF16)
        cc_sb = SS("pd_cc", [128, 2 * 128], BF16)
        zrow = SS("pd_zrow", [1, 2 * 128], F32)
        ps_sc = [PS(f"pd_ps_sc{i}", [128, 512], F32) for i in range(2)]
        ps_cx = [PS(f"pd_ps_cx{i}", [128, 512], F32) for i in range(2)]
        ps_zb = [PS(f"pd_ps_zb{i}", [128, 128], F32) for i in range(2)]
        ps_o = [PS(f"pd_ps_o{i}", [128, 512], F32) for i in range(2)]

        for dc in range(DC):
            k.dma("g", out=woT_b[:, dc * D:(dc + 1) * D],
                  in_=W["w_outT"].rearrange("(c p) f -> p c f", p=128)[:, dc, :])
        dwo = ("d", "g", k.dcnt["g"])
        d2 = k.dma("q", out=brow[:, :], in_=W["b_out_row"])
        k.wait("t", d2)
        emit_bcast_row(k, bout_bc, brow[0:1, 0:D], ps_o[0], D)

        vt = v_tm.ap().rearrange("p (n h c) -> p n h c", n=NT, h=H0)

        def qoff(h):
            return (h % 2) * 64, h // 2

        exp_toks = {}
        sc_ctr = [0]

        def emit_scores(kt):
            slot = kt % 3
            tvlast = None
            for h in range(H0):
                pb, ch = qoff(h)
                ps = ps_sc[h % 2]
                sc = sc_ctr[0]
                sc_ctr[0] += 1
                k.wait("t", exp_toks.get(sc - 2))
                k.op("t", "matmul", out=ps[:, 0:WIN],
                     lhsT=k_fm[pb:pb + HD0, ch * T + kt * 128: ch * T + (kt + 1) * 128],
                     rhs=q_ext[pb:pb + HD0, ch * (T + 16) + kt * 128: ch * (T + 16) + kt * 128 + WIN],
                     start=True, stop=True)
                tt = k.inc("t")
                k.wait("s", tt)
                dst = E_sb[:, (slot * H0 + h) * WIN:(slot * H0 + h + 1) * WIN]
                k.op("s", "activation", out=dst, in_=ps[:, 0:WIN], func=AF.Exp, bias=C.m12[:], scale=0.125)
                exp_toks[sc] = k.inc("s")
                k.wait("v", exp_toks[sc])
                k.op("v", "tensor_tensor", out=dst, in0=dst, in1=C.band_b[:], op=ALU.mult)
                tvlast = k.inc("v")
            return tvlast

        zb_toks = {}
        cc_toks = {}
        gh_ctr = [0]

        def ctx_grp_mm(qt, grp, depE):
            ps = ps_cx[grp % 2]
            prevg = (qt * 4 + grp) - 2
            k.wait("t", depE, cc_toks.get(("grp", prevg)))
            for hh in range(4):
                h = grp * 4 + hh
                cols = hh * 128
                k.op("t", "matmul", out=ps[0:HD0 + 1, cols:cols + 128],
                     lhsT=vt[:, qt, h, :],
                     rhs=E_sb[:, ((qt % 3) * H0 + h) * WIN + 8:((qt % 3) * H0 + h) * WIN + 136],
                     start=True, stop=True, skip_group_check=True)
                if qt - 1 >= 0:
                    sl = (qt - 1) % 3
                    k.op("t", "matmul", out=ps[0:HD0 + 1, cols:cols + 8],
                         lhsT=vt[:, qt - 1, h, :],
                         rhs=E_sb[:, (sl * H0 + h) * WIN + 136:(sl * H0 + h) * WIN + 144],
                         start=False, stop=True, skip_group_check=True)
                if qt + 1 < NT:
                    sl = (qt + 1) % 3
                    k.op("t", "matmul", out=ps[0:HD0 + 1, cols + 120:cols + 128],
                         lhsT=vt[:, qt + 1, h, :],
                         rhs=E_sb[:, (sl * H0 + h) * WIN:(sl * H0 + h) * WIN + 8],
                         start=False, stop=True, skip_group_check=True)
            return k.inc("t")

        def ctx_grp_evac(qt, grp, tmm):
            ps = ps_cx[grp % 2]
            for hh in range(4):
                h = grp * 4 + hh
                cols = hh * 128
                pb, ch = qoff(h)
                gh = gh_ctr[0]
                gh_ctr[0] += 1
                k.wait("v", tmm, zb_toks.get(gh - 2))
                k.op("v", "reciprocal", out=zrow[0:1, (h % 2) * 128:(h % 2) * 128 + 128],
                     in_=ps[HD0:HD0 + 1, cols:cols + 128])
                tz = k.inc("v")
                k.wait("t", tz, cc_toks.get(gh - 2))
                k.op("t", "matmul", out=ps_zb[h % 2][0:HD0, 0:128],
                     lhsT=C.ones_row_f[0:1, 0:HD0],
                     rhs=zrow[0:1, (h % 2) * 128:(h % 2) * 128 + 128],
                     start=True, stop=True, skip_group_check=True)
                zb_toks[gh] = k.inc("t")
                k.wait("s", tmm, cc_toks.get(gh - 2))
                k.op("s", "copy", out=cc_sb[pb:pb + HD0, (h % 2) * 128:(h % 2) * 128 + 128],
                     in_=ps[0:HD0, cols:cols + 128])
                tcc = k.inc("s")
                k.wait("v", tcc, zb_toks[gh])
                k.op("v", "tensor_tensor", out=ctx_fm[pb:pb + HD0, ch * 128:(ch + 1) * 128],
                     in0=cc_sb[pb:pb + HD0, (h % 2) * 128:(h % 2) * 128 + 128],
                     in1=ps_zb[h % 2][0:HD0, 0:128], op=ALU.mult)
                cc_toks[gh] = k.inc("v")
            cc_toks[("grp", qt * 4 + grp)] = cc_toks[gh_ctr[0] - 1]

        def emit_ctx(qt, depE):
            t0 = ctx_grp_mm(qt, 0, depE)
            t1 = ctx_grp_mm(qt, 1, depE)
            ctx_grp_evac(qt, 0, t0)
            t2 = ctx_grp_mm(qt, 2, depE)
            ctx_grp_evac(qt, 1, t1)
            t3 = ctx_grp_mm(qt, 3, depE)
            ctx_grp_evac(qt, 2, t2)
            ctx_grp_evac(qt, 3, t3)
            k.wait("t", cc_toks[gh_ctr[0] - 1], dwo)
            for half in range(2):
                ps = ps_o[half]
                for dc in range(DC):
                    k.op("t", "matmul", out=ps[:, 0:512],
                         lhsT=ctx_fm[:, dc * 128:(dc + 1) * 128],
                         rhs=woT_b[:, dc * D + half * 512: dc * D + (half + 1) * 512],
                         start=(dc == 0), stop=(dc == DC - 1))
                tt = k.inc("t")
                k.wait("v", tt)
                k.op("v", "tensor_tensor", out=s_tm[:, qt * D + half * 512: qt * D + (half + 1) * 512],
                     in0=ps[:, 0:512], in1=bout_bc[:, half * 512:(half + 1) * 512], op=ALU.add)
                tv = k.inc("v")
                k.wait("t", tv)

        tE = emit_scores(0)
        tE = emit_scores(1)
        for qt in range(NT):
            if qt + 1 < NT and qt + 1 >= 2:
                tE = emit_scores(qt + 1)
            emit_ctx(qt, tE)
        k.fence_all()

    attn_es.close()

    if dbg and "s_tm" in dbg:
        k.dma("q", out=dbg["s_tm"], in_=s_tm.ap())
        k.fence_all()

    with contextlib.ExitStack() as pes:
        PS = lambda *a, **kw: pes.enter_context(nc.psum_tensor(*a, **kw))
        SS = lambda *a, **kw: pes.enter_context(nc.sbuf_tensor(*a, **kw))
        hg = SS("pe_hg", [128, 32 * 512], BF16)
        w1t_b = [SS(f"pe_w1t{i}", [128, DC * 128], BF16) for i in range(2)]
        w2t_b = [SS(f"pe_w2t{i}", [128, D], BF16) for i in range(2)]
        gtmp = SS("pe_gtmp", [128, D], F32)
        g0 = SS("pe_g0", [128, D], BF16)
        yt = SS("pe_yt", [128, D], F32)
        b1_c = SS("pe_b1_c", [128, 32], F32)
        b2_bc = SS("pe_b2_bc", [128, D], F32)
        ps_h = [PS(f"pe_ps_h{i}", [128, 512], F32) for i in range(2)]
        ps_y6 = [PS(f"pe_ps_y{i}", [128, 512], F32) for i in range(6)]
        ps_y = ps_y6 + ps_h

        w1scr_r = W["w1t_scr"].rearrange("(c p) f -> p c f", p=128)
        w2scr_r = W["w2t_scr"].rearrange("(c p) f -> p c f", p=128)

        k.dma("q", out=b1_c[:, :], in_=W["b1_c"])
        d3 = k.dma("q", out=brow[:, :], in_=W["b2_row"])
        k.wait("t", d3)
        emit_bcast_row(k, b2_bc, brow[0:1, 0:D], ps_h[0], D)

        hload = {}
        hsev = {}
        for tsb in range(4):
            for ffc in range(32):
                gffc = tsb * 32 + ffc
                wb = w1t_b[ffc % 2]
                dd = k.dma("q", out=wb.ap().rearrange("p (c f) -> p c f", c=DC),
                           in_=w1scr_r[:, :, ffc * 128:(ffc + 1) * 128],
                           wait=(hload.get(gffc - 2),))
                ps = ps_h[ffc % 2]
                k.wait("t", dd, hsev.get(gffc - 2))
                for dc in range(DC):
                    k.op("t", "matmul", out=ps[:, 0:512],
                         lhsT=wb[:, dc * 128:(dc + 1) * 128],
                         rhs=xn_fm[:, dc * T + tsb * 512: dc * T + (tsb + 1) * 512],
                         start=(dc == 0), stop=(dc == DC - 1))
                tt = k.inc("t")
                hload[gffc] = tt
                k.wait("s", tt)
                k.op("s", "activation", out=hg[:, ffc * 512:(ffc + 1) * 512], in_=ps[:, 0:512],
                     func=AF.Gelu, bias=b1_c[:, ffc:ffc + 1], scale=1.0)
                hsev[gffc] = k.inc("s")
            k.wait("t", hsev[tsb * 32 + 31])
            w2load = {}
            for ffc in range(32):
                wb = w2t_b[ffc % 2]
                dd = k.dma("q", out=wb[:, :], in_=w2scr_r[:, ffc, :], wait=(w2load.get(ffc - 2),))
                k.wait("t", dd)
                for tt_i in range(4):
                    for half in range(2):
                        k.op("t", "matmul", out=ps_y[tt_i * 2 + half][:, 0:512],
                             lhsT=hg[:, ffc * 512 + tt_i * 128: ffc * 512 + (tt_i + 1) * 128],
                             rhs=wb[:, half * 512:(half + 1) * 512],
                             start=(ffc == 0), stop=(ffc == 31))
                w2load[ffc] = k.inc("t")
            tw2 = k.tok("t")
            for tt_i in range(4):
                gi = tsb * 4 + tt_i
                dd = k.dma("q", out=yt[:, :], in_=x_r[:, gi, :], wait=(k.tok("v"),))
                for half in range(2):
                    ps = ps_y[tt_i * 2 + half]
                    k.wait("v", tw2)
                    k.op("v", "tensor_tensor", out=gtmp[:, half * 512:(half + 1) * 512],
                         in0=ps[:, 0:512], in1=b2_bc[:, half * 512:(half + 1) * 512], op=ALU.add)
                tv = k.inc("v")
                k.wait("s", tv)
                k.op("s", "activation", out=g0[:, :], in_=gtmp[:, :], func=AF.Silu)
                ts = k.inc("s")
                k.wait("v", ts, dd)
                k.op("v", "tensor_tensor", out=g0[:, :], in0=g0[:, :],
                     in1=s_tm[:, gi * D:(gi + 1) * D], op=ALU.mult)
                tv = k.inc("v")
                k.wait("v", tv)
                k.op("v", "tensor_tensor", out=yt[:, :], in0=yt[:, :], in1=g0[:, :], op=ALU.add)
                tv = k.inc("v")
                k.dma("q", out=x1_r[:, gi, :], in_=yt[:, :], wait=(tv,))
                k.wait("v", ("d", "q", k.dcnt["q"]))
            k.wait("s", k.tok("t"))
        k.fence_all()
    l0es.close()


TS = T + 1


def emit_l1_ln(k, W, x1_in, xn1_fm, xn_lo_fm, dbg=None):
    nc, es = k.nc, k.es
    x1_r = x1_in.rearrange("(n p) d -> p n d", p=128)
    with contextlib.ExitStack() as pes:
        PS = lambda *a, **kw: pes.enter_context(nc.psum_tensor(*a, **kw))
        SS = lambda *a, **kw: pes.enter_context(nc.sbuf_tensor(*a, **kw))
        ps_ta = PS("pf_ps_ta", [128, 128], BF16)
        ps_tb = PS("pf_ps_tb", [128, 128], BF16)
        xt = [SS(f"pf_xt{i}", [128, D], F32) for i in range(2)]
        xf2 = [SS(f"pf_xf2{i}", [128, D], F32) for i in range(2)]
        xnt = [SS(f"pf_xnt{i}", [128, D], BF16) for i in range(2)]
        xerr = [SS(f"pf_xerr{i}", [128, D], BF16) for i in range(2)]
        stats = SS("pf_stats", [128, 34], F32)
        xnv = xn1_fm.ap().rearrange("p (c n) -> p c n", c=DC)
        k.op("v", "memset", xnv[:, :, 0:1], 0.0)
        k.inc("v")
        prev = [None, None]
        for i in range(NT):
            p = i % 2
            dd = k.dma("q", out=xt[p][:, :], in_=x1_r[:, i, :], wait=(prev[p],))
            k.wait("v", dd)
            tln = emit_ln_tile(k, xt[p], xnt[p][:, :], stats[:, p * 17:p * 17 + 17])
            k.wait("v", tln)
            k.op("v", "tensor_scalar", out=xf2[p][:, :], in0=xt[p][:, 0:D],
                 scalar1=stats[:, p * 17 + 14:p * 17 + 15],
                 scalar2=stats[:, p * 17 + 16:p * 17 + 17], op0=ALU.add, op1=ALU.mult)
            terr0 = k.inc("v")
            k.wait("v", terr0)
            k.op("v", "tensor_tensor", out=xerr[p][:, :], in0=xf2[p][:, :], in1=xnt[p][:, :],
                 op=ALU.subtract)
            terr = k.inc("v")
            emit_transpose_tile(
                k, xnt[p], lambda dc, i=i: xn1_fm[:, dc * TS + 1 + i * 128: dc * TS + 1 + (i + 1) * 128],
                ps_ta, ps_tb, dep=tln)
            t2 = emit_transpose_tile(
                k, xerr[p], lambda dc, i=i: xn_lo_fm[:, dc * T + i * 128: dc * T + (i + 1) * 128],
                ps_ta, ps_tb, dep=terr)
            prev[p] = t2
        k.fence_all()


def emit_l1_gate(k, W, xn1_fm, xn_lo_fm, dw_sb, dbg=None):
    nc = k.nc
    C = k.C
    with contextlib.ExitStack() as pes:
        PS = lambda *a, **kw: pes.enter_context(nc.psum_tensor(*a, **kw))
        SS = lambda *a, **kw: pes.enter_context(nc.sbuf_tensor(*a, **kw))
        gw_f = SS("pg_gw_f", [128, DC * NE], F32)
        gw_hi = SS("pg_gw_hi", [128, DC * NE], BF16)
        gw_lo = SS("pg_gw_lo", [128, DC * NE], BF16)
        gerr = SS("pg_gerr", [128, DC * NE], F32)
        lg = SS("pg_lg", [128, NE], F32)
        ab = SS("pg_ab", [128, NE], F32)
        u1 = SS("pg_u1", [128, NE], F32)
        m1 = SS("pg_m1", [128, NE], F32)
        e1 = SS("pg_e1", [128, NE], F32)
        x1s = SS("pg_x1s", [128, NE], F32)
        oh1 = SS("pg_oh1", [128, NE], F32)
        s2 = SS("pg_s2", [128, NE], F32)
        oh2 = SS("pg_oh2", [128, NE], F32)
        cols = SS("pg_cols", [128, 16], F32)
        ps_g = [PS(f"pg_ps{i}", [128, 512], F32) for i in range(2)]

        dd = k.dma("q", out=gw_f.ap().rearrange("p (c e) -> p c e", c=DC),
                   in_=W["gate_wT"].rearrange("(c p) e -> p c e", p=128))
        k.wait("v", dd)
        k.op("v", "tensor_copy", out=gw_hi[:, :], in_=gw_f[:, :])
        tv = k.inc("v")
        k.wait("v", tv)
        k.op("v", "tensor_tensor", out=gerr[:, :], in0=gw_f[:, :], in1=gw_hi[:, :], op=ALU.subtract)
        tv = k.inc("v")
        k.wait("v", tv)
        k.op("v", "tensor_copy", out=gw_lo[:, :], in_=gerr[:, :])
        tglo = k.inc("v")
        k.wait("t", tglo)

        prev_read = {}
        for ti in range(NT):
            ps = ps_g[ti % 2]
            k.wait("t", prev_read.get(ti - 2))
            nmm = 3 * DC
            idx = 0
            for dc in range(DC):
                hi_x = xn1_fm[:, dc * TS + 1 + ti * 128: dc * TS + 1 + (ti + 1) * 128]
                lo_x = xn_lo_fm[:, dc * T + ti * 128: dc * T + (ti + 1) * 128]
                hi_w = gw_hi[:, dc * NE:(dc + 1) * NE]
                lo_w = gw_lo[:, dc * NE:(dc + 1) * NE]
                for lx, lw in ((hi_x, hi_w), (lo_x, hi_w), (hi_x, lo_w)):
                    k.op("t", "matmul", out=ps[:, 0:NE], lhsT=lx, rhs=lw,
                         start=(idx == 0), stop=(idx == nmm - 1))
                    idx += 1
            tt = k.inc("t")
            v = "v"
            k.wait(v, tt)
            k.op(v, "tensor_copy", out=lg[:, :], in_=ps[:, 0:NE])
            tl = k.inc(v)
            prev_read[ti] = tl
            k.wait(v, tl)
            k.op(v, "reduce_max", out=cols[:, 0:1], in_=lg[:, :], axis=AX.X)
            t_ = k.inc(v); k.wait(v, t_)
            k.op(v, "tensor_single_scalar", out=ab[:, :], in_=lg[:, :], scalar=-1.0, op=ALU.mult)
            t_ = k.inc(v); k.wait(v, t_)
            k.op(v, "tensor_tensor", out=ab[:, :], in0=ab[:, :], in1=lg[:, :], op=ALU.max)
            t_ = k.inc(v); k.wait(v, t_)
            k.op(v, "tensor_scalar", out=oh1[:, :], in0=lg[:, :], scalar1=cols[:, 0:1],
                 scalar2=None, op0=ALU.is_equal, op1=ALU.bypass)
            t_ = k.inc(v); k.wait(v, t_)
            k.op(v, "tensor_scalar", out=u1[:, :], in0=lg[:, :], scalar1=cols[:, 0:1],
                 scalar2=-1.0, op0=ALU.subtract, op1=ALU.mult)
            t_ = k.inc(v); k.wait(v, t_)
            k.op(v, "tensor_scalar", out=m1[:, :], in0=ab[:, :], scalar1=cols[:, 0:1],
                 scalar2=-JIT2, op0=ALU.max, op1=ALU.mult)
            t_ = k.inc(v); k.wait(v, t_)
            k.op(v, "tensor_tensor", out=m1[:, :], in0=u1[:, :], in1=m1[:, :], op=ALU.add)
            t_ = k.inc(v); k.wait(v, t_)
            k.op(v, "tensor_single_scalar", out=m1[:, :], in_=m1[:, :], scalar=0.0, op=ALU.is_gt)
            t_ = k.inc(v); k.wait(v, t_)
            k.op(v, "scalar_tensor_tensor", out=e1[:, :], in0=m1[:, :], scalar=NEG, in1=lg[:, :],
                 op0=ALU.mult, op1=ALU.add)
            t_ = k.inc(v)
            k.op(v, "tensor_single_scalar", out=cols[:, 1:2], in_=cols[:, 0:1], scalar=-1.0, op=ALU.mult)
            t_ = k.inc(v); k.wait("s", t_)
            k.op("s", "activation", out=x1s[:, :], in_=e1[:, :], func=AF.Exp, bias=cols[:, 1:2], scale=1.0)
            ts_ = k.inc("s")
            k.wait(v, ts_)
            k.op(v, "reduce_sum", out=cols[:, 3:4], in_=x1s[:, :], axis=AX.X)
            t_ = k.inc(v); k.wait(v, t_)
            k.op(v, "tensor_tensor", out=e1[:, :], in0=x1s[:, :], in1=oh1[:, :], op=ALU.mult)
            t_ = k.inc(v); k.wait(v, t_)
            k.op(v, "reduce_sum", out=cols[:, 4:5], in_=e1[:, :], axis=AX.X)
            t_ = k.inc(v); k.wait(v, t_)
            k.op(v, "reciprocal", out=cols[:, 5:6], in_=cols[:, 3:4])
            t_ = k.inc(v); k.wait(v, t_)
            k.op(v, "tensor_tensor", out=cols[:, 5:6], in0=cols[:, 4:5], in1=cols[:, 5:6], op=ALU.mult)
            t_ = k.inc(v); k.wait(v, t_)
            k.op(v, "scalar_tensor_tensor", out=s2[:, :], in0=oh1[:, :], scalar=NEG, in1=lg[:, :],
                 op0=ALU.mult, op1=ALU.add)
            t_ = k.inc(v); k.wait(v, t_)
            k.op(v, "reduce_max", out=cols[:, 8:9], in_=s2[:, :], axis=AX.X)
            t_ = k.inc(v); k.wait(v, t_)
            k.op(v, "tensor_scalar", out=oh2[:, :], in0=s2[:, :], scalar1=cols[:, 8:9],
                 scalar2=None, op0=ALU.is_equal, op1=ALU.bypass)
            t_ = k.inc(v); k.wait(v, t_)
            k.op(v, "tensor_scalar", out=u1[:, :], in0=lg[:, :], scalar1=cols[:, 8:9],
                 scalar2=-1.0, op0=ALU.subtract, op1=ALU.mult)
            t_ = k.inc(v); k.wait(v, t_)
            k.op(v, "tensor_scalar", out=m1[:, :], in0=ab[:, :], scalar1=cols[:, 8:9],
                 scalar2=-JIT2, op0=ALU.max, op1=ALU.mult)
            t_ = k.inc(v); k.wait(v, t_)
            k.op(v, "tensor_tensor", out=m1[:, :], in0=u1[:, :], in1=m1[:, :], op=ALU.add)
            t_ = k.inc(v); k.wait(v, t_)
            k.op(v, "tensor_single_scalar", out=m1[:, :], in_=m1[:, :], scalar=0.0, op=ALU.is_gt)
            t_ = k.inc(v); k.wait(v, t_)
            k.op(v, "scalar_tensor_tensor", out=e1[:, :], in0=m1[:, :], scalar=NEG, in1=s2[:, :],
                 op0=ALU.mult, op1=ALU.add)
            t_ = k.inc(v)
            k.op(v, "tensor_single_scalar", out=cols[:, 9:10], in_=cols[:, 8:9], scalar=-1.0, op=ALU.mult)
            t_ = k.inc(v); k.wait("s", t_)
            k.op("s", "activation", out=x1s[:, :], in_=e1[:, :], func=AF.Exp, bias=cols[:, 9:10], scale=1.0)
            ts_ = k.inc("s")
            k.wait(v, ts_)
            k.op(v, "reduce_sum", out=cols[:, 11:12], in_=x1s[:, :], axis=AX.X)
            t_ = k.inc(v); k.wait(v, t_)
            k.op(v, "tensor_tensor", out=e1[:, :], in0=x1s[:, :], in1=oh2[:, :], op=ALU.mult)
            t_ = k.inc(v); k.wait(v, t_)
            k.op(v, "reduce_sum", out=cols[:, 12:13], in_=e1[:, :], axis=AX.X)
            t_ = k.inc(v); k.wait(v, t_)
            k.op(v, "reciprocal", out=cols[:, 13:14], in_=cols[:, 11:12])
            t_ = k.inc(v); k.wait(v, t_)
            k.op(v, "tensor_tensor", out=cols[:, 13:14], in0=cols[:, 12:13], in1=cols[:, 13:14], op=ALU.mult)
            t_ = k.inc(v); k.wait(v, t_)
            k.op(v, "tensor_scalar", out=oh1[:, :], in0=oh1[:, :], scalar1=cols[:, 5:6],
                 scalar2=None, op0=ALU.mult, op1=ALU.bypass)
            t_ = k.inc(v); k.wait(v, t_)
            k.op(v, "tensor_scalar", out=oh2[:, :], in0=oh2[:, :], scalar1=cols[:, 13:14],
                 scalar2=None, op0=ALU.mult, op1=ALU.bypass)
            t_ = k.inc(v); k.wait(v, t_)
            k.op(v, "tensor_tensor", out=dw_sb[:, ti * NE:(ti + 1) * NE], in0=oh1[:, :],
                 in1=oh2[:, :], op=ALU.add)
            k.inc(v)
        k.fence_all()


def emit_l1_moe(k, W, xn1_fm, dw_sb, ymoe_out, dbg=None):
    nc = k.nc
    C = k.C
    with contextlib.ExitStack() as pes:
        PS = lambda *a, **kw: pes.enter_context(nc.psum_tensor(*a, **kw))
        SS = lambda *a, **kw: pes.enter_context(nc.sbuf_tensor(*a, **kw))
        Y_sb = SS("ph_Y", [128, NT * D], F32)
        act = SS("ph_act", [128, 16 * 1024], BF16)
        w2_sb = SS("ph_w2", [128, 16 * D], BF16)
        w13 = [SS(f"ph_w13_{i}", [128, DC * 128], BF16) for i in range(4)]
        stmp = SS("ph_stmp", [128, 1024], BF16)
        ps_h1 = [PS(f"ph_ps_h1{i}", [128, 512], F32) for i in range(2)]
        ps_h3 = [PS(f"ph_ps_h3{i}", [128, 512], F32) for i in range(2)]
        ps_y = [PS(f"ph_ps_y{i}", [128, 512], F32) for i in range(4)]

        k.op("v", "memset", Y_sb[:, :], 0.0)
        k.inc("v")
        w1r = W["moe_w1T"].rearrange("e (c p) f -> p e c f", p=128)
        w3r = W["moe_w3T"].rearrange("e (c p) f -> p e c f", p=128)
        w2r = W["moe_w2T"].rearrange("e (c p) f -> p e c f", p=128)

        wtoks = {}
        ytoks = {}
        hev = {}
        ypsum_tok = {}
        yct = [0]
        fct = [0]
        for e in range(MOE_EXPERTS):
            dw2 = k.dma("g", out=w2_sb.ap().rearrange("p (c f) -> p c f", c=16),
                        in_=w2r[:, e, :, :], wait=(ytoks.get(e - 1),))
            if MOE_FENCE_E:
                k.fence_all()
            for tsb in range(2):
                tb0 = tsb * 1024
                if MOE_FENCE_TSB and tsb == 1:
                    k.fence_all()
                k.wait("v", k.tok("t"))
                k.wait("s", k.tok("t"))
                for ffc in range(16):
                    fc = fct[0]
                    fct[0] += 1
                    b1 = w13[(fc % 2) * 2]
                    b3 = w13[(fc % 2) * 2 + 1]
                    dd1 = k.dma("g", out=b1.ap().rearrange("p (c f) -> p c f", c=DC),
                                in_=w1r[:, e, :, ffc * 128:(ffc + 1) * 128],
                                wait=(wtoks.get(fc - 2),))
                    dd3 = k.dma("g", out=b3.ap().rearrange("p (c f) -> p c f", c=DC),
                                in_=w3r[:, e, :, ffc * 128:(ffc + 1) * 128])
                    k.wait("t", dd1, dd3)
                    for sub in range(2):
                        c0 = tb0 + sub * 512
                        ph1 = ps_h1[sub]
                        ph3 = ps_h3[sub]
                        k.wait("t", hev.get((fc - 1, sub)))
                        for dc in range(DC):
                            rhsx = xn1_fm.ap().rearrange("p (c n) -> p c n", c=DC)[:, dc, 1 + c0:1 + c0 + 512]
                            k.op("t", "matmul", out=ph1[:, 0:512],
                                 lhsT=b1[:, dc * 128:(dc + 1) * 128], rhs=rhsx,
                                 start=(dc == 0), stop=(dc == DC - 1))
                        for dc in range(DC):
                            rhsx = xn1_fm.ap().rearrange("p (c n) -> p c n", c=DC)[:, dc, 1 + c0:1 + c0 + 512]
                            k.op("t", "matmul", out=ph3[:, 0:512],
                                 lhsT=b3[:, dc * 128:(dc + 1) * 128], rhs=rhsx,
                                 start=(dc == 0), stop=(dc == DC - 1))
                        tmm = k.inc("t")
                        k.wait("s", tmm, hev.get((fc - 1, sub)))
                        k.op("s", "activation", out=stmp[:, sub * 512:(sub + 1) * 512],
                             in_=ph1[:, 0:512], func=AF.Silu)
                        ts_ = k.inc("s")
                        k.wait("v", ts_)
                        k.op("v", "tensor_tensor", out=act[:, ffc * 1024 + sub * 512: ffc * 1024 + (sub + 1) * 512],
                             in0=stmp[:, sub * 512:(sub + 1) * 512], in1=ph3[:, 0:512], op=ALU.mult)
                        hev[(fc, sub)] = k.inc("v")
                    if MOE_FENCE_H:
                        k.fence_all()
                    wtoks[fc] = hev[(fc, 1)]
                k.wait("t", dw2, wtoks[fct[0] - 1])
                for tt_loc in range(8):
                    ti = tsb * 8 + tt_loc
                    py = [ps_y[(tt_loc % 2) * 2], ps_y[(tt_loc % 2) * 2 + 1]]
                    yc = yct[0]
                    yct[0] += 1
                    k.wait("t", ypsum_tok.get(yc - 2))
                    for ffc in range(16):
                        for half in range(2):
                            k.op("t", "matmul", out=py[half][:, 0:512],
                                 lhsT=act[:, ffc * 1024 + tt_loc * 128: ffc * 1024 + (tt_loc + 1) * 128],
                                 rhs=w2_sb[:, ffc * D + half * 512: ffc * D + (half + 1) * 512],
                                 start=(ffc == 0), stop=(ffc == 15))
                    tym = k.inc("t")
                    k.wait("v", tym)
                    for half in range(2):
                        k.op("v", "scalar_tensor_tensor",
                             out=Y_sb[:, ti * D + half * 512: ti * D + (half + 1) * 512],
                             in0=py[half][:, 0:512], scalar=dw_sb[:, ti * NE + e: ti * NE + e + 1],
                             in1=Y_sb[:, ti * D + half * 512: ti * D + (half + 1) * 512],
                             op0=ALU.mult, op1=ALU.add)
                    tv = k.inc("v")
                    ypsum_tok[yc] = tv
                    if MOE_FENCE_Y:
                        k.fence_all()
            ytoks[e] = k.tok("t")
        k.fence_all()
        ym_r = ymoe_out.rearrange("(n p) d -> p n d", p=128)
        for i in range(NT):
            k.dma("q", out=ym_r[:, i, :], in_=Y_sb[:, i * D:(i + 1) * D])
        k.fence_all()


def emit_l1_scan(k, W, x1_in, ymoe_in, y_out, xn1_fm, dbg=None):
    nc, es = k.nc, k.es
    C = k.C
    x1_r = x1_in.rearrange("(n p) d -> p n d", p=128)
    ym_r = ymoe_in.rearrange("(n p) d -> p n d", p=128)
    y_r = y_out.rearrange("(n p) d -> p n d", p=128)
    CH = 128
    NCH = T // CH
    SUB = 32
    NSUB = CH // SUB
    CL = CH + 1

    with contextlib.ExitStack() as pes:
        PS = lambda *a, **kw: pes.enter_context(nc.psum_tensor(*a, **kw))
        SS = lambda *a, **kw: pes.enter_context(nc.sbuf_tensor(*a, **kw))
        vw = SS("pi_vw", [128, DC * VD], BF16)
        gw = SS("pi_gw", [128, DC * VD], BF16)
        ow = SS("pi_ow", [128, DC * D], BF16)
        rw = SS("pi_rw", [128, DC * KD], BF16)
        kw_ = SS("pi_kw", [128, DC * KD], BF16)
        ww1 = SS("pi_ww1", [128, DC * GD], BF16)
        ww2 = SS("pi_ww2", [64, KD], BF16)
        xlw1 = SS("pi_xlw1", [128, DC * RLO], BF16)
        xlw2 = SS("pi_xlw2", [32, R5], BF16)
        xow = SS("pi_xow", [33, 5 * D], BF16)
        mu_c = SS("pi_mu_c", [128, DC], F32)
        xlb2_c = SS("pi_xlb2", [128, 2], F32)
        wb_c = SS("pi_wb", [128, NH], F32)
        bon_c = SS("pi_bon", [128, NH], F32)
        gnw_c = SS("pi_gnw", [128, DC], F32)
        gnb_c = SS("pi_gnb", [128, DC], F32)
        Cl = SS("pi_Cl", [128, NH * CL], F32)
        S_f = SS("pi_S_f", [128, NH * DV], F32)
        S_b = SS("pi_S_b", [128, NH * DV], BF16)
        sl_f = SS("pi_sl_f", [128, NH * DV], F32)
        sl_b = SS("pi_sl_b", [128, NH * DV], BF16)
        d1 = SS("pi_d1", [128, DC * CH], BF16)
        xl = SS("pi_xl", [128, DC * CH], BF16)
        x1g_sb = SS("pi_x1g", [33, 5 * CH], BF16)
        x0t = SS("pi_x0t", [32, CH], BF16)
        mu_sb = SS("pi_mu_sb", [128, DC * CH], BF16)
        a_sb = [SS(f"pi_a{i}", [128, DC * CH], BF16) for i in range(5)]
        r_sb = SS("pi_r_sb", [128, KD], BF16)
        k_sb = SS("pi_k_sb", [128, KD], BF16)
        wt64 = SS("pi_wt64", [64, CH], BF16)
        lw_f = SS("pi_lw_f", [128, NH * CH], F32)
        vt_sb = SS("pi_vt", [128, VD], BF16)
        vtj_all = [SS(f"pi_vtj{j}", [32, NH * DV], BF16) for j in range(NSUB)]
        g_sb = SS("pi_g_sb", [128, DC * CH], BF16)
        rt_all = SS("pi_rt", [128, KD], BF16)
        rb_all = SS("pi_rb", [128, KD], BF16)
        fh2_sb = SS("pi_fh2", [128, KD], F32)
        f_sc = SS("pi_fsc", [128, 4 * NH * SUB], F32)
        rh_all = SS("pi_rh", [128, NH * SUB], BF16)
        kh_all = SS("pi_kh", [128, NH * SUB], BF16)
        k3_all = SS("pi_k3", [128, NH * SUB], BF16)
        rbk_all = SS("pi_rbk", [128, NH * SUB], BF16)
        P_sb = SS("pi_P", [32, NH * SUB], BF16)
        dgd_sb = SS("pi_dgd", [32, NH * SUB], BF16)
        dg_sb = SS("pi_dg", [1, NH * SUB], BF16)
        k3t_sb = SS("pi_k3t", [32, NH * 128], BF16)
        k2_sb = SS("pi_k2", [128, KD], BF16)
        k2t_sb = SS("pi_k2t", [128, KD], BF16)
        ecols = SS("pi_ecols", [128, 16], F32)
        o_sb = SS("pi_o_sb", [128, NH * DV], BF16)
        o2_sb = SS("pi_o2", [128, NH * DV], BF16)
        on_sb = SS("pi_on", [128, DC * CH], BF16)
        gnrow = SS("pi_gnrow", [1, 4 * 512], F32)
        xt_f = SS("pi_xt_f", [128, D], F32)
        ymt = SS("pi_ymt", [128, D], F32)
        g1t = SS("pi_g1t", [128, D], BF16)
        ps_big = PS("pi_ps_big", [128, 1024], F32)
        ps_pr = PS("pi_ps_pr", [128, 512], F32)
        ps_Oa = PS("pi_ps_Oa", [128, 512], F32)
        ps_Ob = PS("pi_ps_Ob", [128, 512], F32)
        ps_tp = PS("pi_ps_tp", [128, 512], BF16)
        ps_u = PS("pi_ps_u", [128, 1024], F32)

        def ldw(dst, nm, ncol, nch=DC):
            return k.dma("g", out=dst.ap().rearrange("p (c f) -> p c f", c=nch),
                         in_=W[nm].rearrange("(c p) f -> p c f", p=128))
        ldw(vw, "v_wT", VD)
        ldw(gw, "g_wT", VD)
        ldw(ow, "o_wT", D)
        ldw(rw, "r_wT", KD)
        ldw(kw_, "k_wT", KD)
        ldw(ww1, "w_w1T", GD)
        ldw(xlw1, "xl_w1T", RLO)
        k.dma("g", out=ww2[:, :], in_=W["w_w2T"])
        k.dma("g", out=xlw2[:, :], in_=W["xl_w2T"])
        k.dma("g", out=xow[:, :], in_=W["x_out_wTn"])
        k.dma("q", out=mu_c[:, :], in_=W["x_proj_mu_c"])
        k.dma("q", out=xlb2_c[:, :], in_=W["xlb2_cols"])
        k.dma("q", out=wb_c[:, :], in_=W["w_b_c"])
        k.dma("q", out=bon_c[:, :], in_=W["bonus_c"])
        k.dma("q", out=gnw_c[:, :], in_=W["gn_w_c"])
        k.dma("q", out=gnb_c[:, :], in_=W["gn_b_c"])
        k.fence_all()

        Cl3 = Cl.ap().rearrange("p (h n) -> p h n", h=NH)
        r3 = r_sb.ap().rearrange("p (h n) -> p h n", h=NH)
        k3v = k_sb.ap().rearrange("p (h n) -> p h n", h=NH)
        rb3 = rb_all.ap().rearrange("p (h n) -> p h n", h=NH)
        f4 = f_sc.ap().rearrange("p (a h s) -> p a h s", a=4, h=NH)
        rh3 = rh_all.ap().rearrange("p (h s) -> p h s", h=NH)
        kh3 = kh_all.ap().rearrange("p (h s) -> p h s", h=NH)
        rbk3 = rbk_all.ap().rearrange("p (h s) -> p h s", h=NH)
        fh23 = fh2_sb.ap().rearrange("p (h n) -> p h n", h=NH)

        k.op("v", "memset", Cl3[:, :, 0:1], 0.0)
        k.op("v", "memset", S_f[:, :], 0.0)
        k.op("v", "memset", S_b[:, :], 0.0)
        k.op("v", "memset", x1g_sb[32:33, :], 1.0)
        k.inc("v")
        k.fence_all()

        prev_tC = None
        for ci in range(NCH):
            c0 = ci * CH
            hcol = lambda dc: xn1_fm[:, dc * TS + 1 + c0: dc * TS + 1 + c0 + CH]
            scol = lambda dc: xn1_fm[:, dc * TS + c0: dc * TS + c0 + CH]
            k.wait("v", k.tok("t"))
            k.wait("g", k.tok("t"))
            k.wait("s", k.tok("t"))
            k.wait("s", k.tok("v"))
            k.wait("v", k.tok("s"))

            for dc in range(DC):
                k.op("v", "tensor_tensor", out=d1[:, dc * CH:(dc + 1) * CH], in0=scol(dc),
                     in1=hcol(dc), op=ALU.subtract)
            tv = k.inc("v")
            k.wait("v", tv)
            for dc in range(DC):
                k.op("v", "scalar_tensor_tensor", out=xl[:, dc * CH:(dc + 1) * CH],
                     in0=d1[:, dc * CH:(dc + 1) * CH], scalar=mu_c[:, dc:dc + 1],
                     in1=hcol(dc), op0=ALU.mult, op1=ALU.add)
            txl = k.inc("v")
            k.wait("t", txl)
            for dc in range(DC):
                k.op("t", "matmul", out=ps_pr[0:RLO, 0:CH], lhsT=xlw1[:, dc * RLO:(dc + 1) * RLO],
                     rhs=xl[:, dc * CH:(dc + 1) * CH], start=(dc == 0), stop=(dc == DC - 1))
            tt = k.inc("t")
            k.wait("s", tt)
            k.op("s", "activation", out=x0t[:, :], in_=ps_pr[0:RLO, 0:CH], func=AF.Tanh)
            ts_ = k.inc("s")
            k.wait("t", ts_)
            k.op("t", "matmul", out=ps_pr[0:96, 128:128 + CH], lhsT=xlw2[:, 0:96], rhs=x0t[:, :],
                 start=True, stop=True, skip_group_check=True)
            k.op("t", "matmul", out=ps_pr[0:64, 256:256 + CH], lhsT=xlw2[:, 96:160], rhs=x0t[:, :],
                 start=True, stop=True, skip_group_check=True)
            tt = k.inc("t")
            k.wait("s", tt)
            for n in range(3):
                k.op("s", "activation", out=x1g_sb[0:32, n * CH:(n + 1) * CH],
                     in_=ps_pr[32 * n:32 * (n + 1), 128:128 + CH], func=AF.Tanh,
                     bias=xlb2_c[32 * n:32 * (n + 1), 0:1], scale=1.0)
            for n in range(2):
                k.op("s", "activation", out=x1g_sb[0:32, (3 + n) * CH:(4 + n) * CH],
                     in_=ps_pr[32 * n:32 * (n + 1), 256:256 + CH], func=AF.Tanh,
                     bias=xlb2_c[32 * n:32 * (n + 1), 1:2], scale=1.0)
            ts_ = k.inc("s")
            k.wait("t", ts_)
            x1g = [x1g_sb[0:33, n * CH:(n + 1) * CH] for n in range(5)]
            for n in range(5):
                for dc in range(DC):
                    k.op("t", "matmul", out=ps_big[:, dc * 128:(dc + 1) * 128],
                         lhsT=xow[0:33, n * D + dc * 128: n * D + (dc + 1) * 128], rhs=x1g[n],
                         start=True, stop=True, skip_group_check=True)
                tt = k.inc("t")
                k.wait("s", tt)
                k.op("s", "copy", out=mu_sb[:, :], in_=ps_big[:, 0:DC * CH])
                ts_ = k.inc("s")
                k.wait("v", ts_)
                k.op("v", "tensor_tensor", out=a_sb[n][:, :], in0=mu_sb[:, :], in1=d1[:, :], op=ALU.mult)
                tv = k.inc("v")
                k.wait("g", tv)
                k.op("g", "tensor_tensor",
                     out=a_sb[n].ap().rearrange("p (c n) -> p c n", c=DC),
                     in0=a_sb[n].ap().rearrange("p (c n) -> p c n", c=DC),
                     in1=xn1_fm.ap().rearrange("p (c n) -> p c n", c=DC)[:, :, 1 + c0:1 + c0 + CH],
                     op=ALU.add)
                tg = k.inc("g")
                k.wait("t", tg)
                k.wait("s", tv)
            tlerp = k.tok("g")
            for (asrc, wsrc, dst) in ((a_sb[0], rw, r_sb), (a_sb[2], kw_, k_sb)):
                k.wait("t", tlerp)
                for kc in range(4):
                    for dc in range(DC):
                        k.op("t", "matmul", out=ps_pr[:, kc * 128:(kc + 1) * 128],
                             lhsT=wsrc[:, dc * KD + kc * 128: dc * KD + (kc + 1) * 128],
                             rhs=asrc[:, dc * CH:(dc + 1) * CH],
                             start=(dc == 0), stop=(dc == DC - 1))
                tt = k.inc("t")
                k.wait("s", tt)
                k.op("s", "copy", out=dst[:, :], in_=ps_pr[:, 0:KD])
                ts_ = k.inc("s")
                k.wait("t", ts_)
            tok_rk = k.tok("s")
            for dc in range(DC):
                k.op("t", "matmul", out=ps_pr[0:GD, 0:CH], lhsT=ww1[:, dc * GD:(dc + 1) * GD],
                     rhs=a_sb[1][:, dc * CH:(dc + 1) * CH], start=(dc == 0), stop=(dc == DC - 1))
            tt = k.inc("t")
            k.wait("s", tt)
            k.op("s", "activation", out=wt64[:, :], in_=ps_pr[0:GD, 0:CH], func=AF.Tanh)
            ts_ = k.inc("s")
            k.wait("t", ts_)
            for kc in range(4):
                k.op("t", "matmul", out=ps_pr[:, kc * 128:(kc + 1) * 128],
                     lhsT=ww2[:, kc * 128:(kc + 1) * 128], rhs=wt64[:, :],
                     start=True, stop=True)
            tt = k.inc("t")
            k.wait("s", tt)
            for kc in range(4):
                k.op("s", "activation", out=lw_f[:, kc * CH:(kc + 1) * CH],
                     in_=ps_pr[:, kc * 128:(kc + 1) * 128], func=AF.Exp,
                     bias=wb_c[:, kc:kc + 1], scale=1.0)
            ts_ = k.inc("s")
            k.wait("v", ts_)
            k.op("v", "tensor_single_scalar", out=lw_f[:, :], in_=lw_f[:, :], scalar=-1.0, op=ALU.mult)
            tv = k.inc("v")
            k.wait("v", tv, prev_tC)
            for h in range(NH):
                k.op("v", "tensor_tensor_scan", out=Cl3[:, h, 1:1 + CH],
                     data0=lw_f[:, h * CH:(h + 1) * CH], data1=lw_f[:, h * CH:(h + 1) * CH],
                     initial=Cl3[:, h, 0:1], op0=ALU.add, op1=ALU.bypass)
            tC = k.inc("v")
            k.wait("t", ts_, tC)
            for half in range(2):
                for dc in range(DC):
                    k.op("t", "matmul", out=ps_big[:, half * 512:(half + 1) * 512],
                         lhsT=a_sb[3][:, dc * CH:(dc + 1) * CH],
                         rhs=vw[:, dc * VD + half * 512: dc * VD + (half + 1) * 512],
                         start=(dc == 0), stop=(dc == DC - 1))
            tt = k.inc("t")
            k.wait("s", tt)
            k.op("s", "copy", out=vt_sb[:, :], in_=ps_big[:, 0:VD])
            tsv = k.inc("s")
            k.wait("t", tsv)
            for dc_o in range(DC):
                for dc in range(DC):
                    k.op("t", "matmul", out=ps_big[:, dc_o * 128:(dc_o + 1) * 128],
                         lhsT=gw[:, dc * VD + dc_o * 128: dc * VD + (dc_o + 1) * 128],
                         rhs=a_sb[4][:, dc * CH:(dc + 1) * CH],
                         start=(dc == 0), stop=(dc == DC - 1))
            tt = k.inc("t")
            k.wait("s", tt)
            k.op("s", "activation", out=g_sb[:, 0:512], in_=ps_big[:, 0:512], func=AF.Silu)
            k.op("s", "activation", out=g_sb[:, 512:1024], in_=ps_big[:, 512:1024], func=AF.Silu)
            tsg = k.inc("s")
            k.wait("t", tsg)

            k.wait("v", tsv)
            for j in range(NSUB):
                k.op("v", "tensor_copy", out=vtj_all[j][:, :],
                     in_=vt_sb[j * SUB:(j + 1) * SUB, :])
            for h in range(NH):
                k.op("v", "tensor_scalar", out=rb_all[:, h * CH:(h + 1) * CH],
                     in0=r_sb[:, h * CH:(h + 1) * CH],
                     scalar1=bon_c[:, h:h + 1], scalar2=None, op0=ALU.mult, op1=ALU.bypass)
            tv_prep = k.inc("v")
            k.op("s", "activation", out=fh23[:, :, 0:CH], in_=Cl3[:, :, 0:CH], func=AF.Exp)
            ts_eI = k.inc("s")
            k.wait("v", ts_eI)
            k.op("v", "tensor_tensor", out=rt_all[:, :], in0=r_sb[:, :], in1=fh2_sb[:, :],
                 op=ALU.mult)
            tv_rt = k.inc("v")
            k.wait("t", tv_rt)
            for h in range(NH):
                psO = ps_Oa if h < 2 else ps_Ob
                for q in range(2):
                    k.op("t", "matmul",
                         out=psO[:, (h % 2) * 256 + q * 128:(h % 2) * 256 + (q + 1) * 128],
                         lhsT=S_b[:, h * DV + q * 128: h * DV + (q + 1) * 128],
                         rhs=rt_all[:, h * CH:(h + 1) * CH],
                         start=(h % 2 == 0 and q == 0), stop=True,
                         skip_group_check=True)
            k.inc("t")
            prev_asm = None
            prev_slb = None
            ts_tp = None
            for j in range(NSUB):
                y0 = j * SUB
                for h in range(NH):
                    k.op("v", "tensor_scalar", out=f4[:, 0, h, :],
                         in0=Cl3[:, h, y0:y0 + SUB], scalar1=Cl3[:, h, y0:y0 + 1],
                         scalar2=None, op0=ALU.subtract, op1=ALU.bypass)
                k.op("v", "tensor_single_scalar", out=f_sc[:, 128:256], in_=f_sc[:, 0:128],
                     scalar=-60.0, op=ALU.max)
                tv_d = k.inc("v")
                k.wait("s", tv_d)
                k.op("s", "activation", out=f_sc[:, 256:384], in_=f_sc[:, 0:128], func=AF.Exp)
                k.op("s", "activation", out=f_sc[:, 384:512], in_=f_sc[:, 128:256],
                     func=AF.Exp, scale=-1.0)
                ts_e = k.inc("s")
                k.wait("v", ts_e)
                k.op("v", "tensor_tensor", out=rh3[:, :, :], in0=r3[:, :, y0:y0 + SUB],
                     in1=f4[:, 2, :, :], op=ALU.mult)
                k.op("v", "tensor_tensor", out=kh3[:, :, :], in0=k3v[:, :, y0:y0 + SUB],
                     in1=f4[:, 3, :, :], op=ALU.mult)
                k.op("v", "tensor_tensor", out=rbk3[:, :, :], in0=rb3[:, :, y0:y0 + SUB],
                     in1=k3v[:, :, y0:y0 + SUB], op=ALU.mult)
                tv_rk = k.inc("v")
                k.wait("t", tv_rk, prev_asm)
                for h in range(NH):
                    k.op("t", "matmul", out=ps_pr[0:SUB, h * SUB:(h + 1) * SUB],
                         lhsT=kh_all[:, h * SUB:(h + 1) * SUB],
                         rhs=rh_all[:, h * SUB:(h + 1) * SUB],
                         start=True, stop=True, skip_group_check=True)
                for h in range(NH):
                    k.op("t", "matmul", out=ps_pr[0:1, 128 + h * SUB:128 + (h + 1) * SUB],
                         lhsT=C.ones_col_b[:], rhs=rbk_all[:, h * SUB:(h + 1) * SUB],
                         start=True, stop=True, skip_group_check=True)
                tP = k.inc("t")
                k.wait("s", tP)
                k.op("s", "copy", out=dg_sb[:, :], in_=ps_pr[0:1, 128:256])
                ts_dg = k.inc("s")
                k.wait("t", ts_dg)
                k.op("t", "matmul", out=ps_pr[0:SUB, 256:256 + NH * SUB],
                     lhsT=C.ones_row_b[0:1, 0:SUB], rhs=dg_sb[0:1, :],
                     start=True, stop=True, skip_group_check=True)
                tdgb = k.inc("t")
                k.wait("v", tdgb)
                k.op("v", "tensor_tensor", out=P_sb[:, :], in0=ps_pr[0:SUB, 0:NH * SUB],
                     in1=C.supper4[:, :], op=ALU.mult)
                k.op("v", "tensor_tensor", out=dgd_sb[:, :], in0=ps_pr[0:SUB, 256:256 + NH * SUB],
                     in1=C.ident4[:, :], op=ALU.mult)
                k.op("v", "tensor_tensor", out=P_sb[:, :], in0=P_sb[:, :], in1=dgd_sb[:, :],
                     op=ALU.add)
                prev_asm = k.inc("v")
                k.wait("t", prev_asm, prev_slb)
                for h in range(NH):
                    psO = ps_Oa if h < 2 else ps_Ob
                    for q in range(2):
                        oc2 = (h % 2) * 256 + q * 128
                        k.op("t", "matmul", out=psO[:, oc2 + y0: oc2 + y0 + SUB],
                             lhsT=vtj_all[j][:, h * DV + q * 128: h * DV + (q + 1) * 128],
                             rhs=P_sb[:, h * SUB:(h + 1) * SUB],
                             start=False, stop=True, skip_group_check=True)
                        if j > 0:
                            k.op("t", "matmul", out=psO[:, oc2 + y0: oc2 + y0 + SUB],
                                 lhsT=sl_b[:, h * DV + q * 128: h * DV + (q + 1) * 128],
                                 rhs=rh_all[:, h * SUB:(h + 1) * SUB],
                                 start=False, stop=True, skip_group_check=True)
                tOj = k.inc("t")
                if j < NSUB - 1:
                    for h in range(NH):
                        k.op("v", "tensor_scalar", out=ecols[:, h:h + 1],
                             in0=Cl3[:, h, y0 + SUB:y0 + SUB + 1],
                             scalar1=Cl3[:, h, y0:y0 + 1],
                             scalar2=None, op0=ALU.subtract, op1=ALU.bypass)
                    tv3 = k.inc("v")
                    k.wait("s", tv3)
                    k.op("s", "activation", out=ecols[:, 4:8], in_=ecols[:, 0:4], func=AF.Exp)
                    ts3 = k.inc("s")
                    k.wait("v", ts3)
                    for h in range(NH):
                        k.op("v", "tensor_scalar", out=k3_all[:, h * SUB:(h + 1) * SUB],
                             in0=kh_all[:, h * SUB:(h + 1) * SUB],
                             scalar1=ecols[:, 4 + h:5 + h], scalar2=None,
                             op0=ALU.mult, op1=ALU.bypass)
                    tvk3 = k.inc("v")
                    k.wait("t", tvk3, ts_tp)
                    for h in range(NH):
                        k.op("t", "transpose", out=ps_tp[0:SUB, h * 128:h * 128 + 128],
                             in_=k3_all[:, h * SUB:(h + 1) * SUB], identity=C.ident_b[:])
                    ttr = k.inc("t")
                    k.wait("s", ttr)
                    k.op("s", "copy", out=k3t_sb[:, :], in_=ps_tp[0:SUB, 0:NH * 128])
                    ts_tp = k.inc("s")
                    k.wait("t", ts_tp)
                    for h in range(NH):
                        k.op("t", "matmul", out=ps_u[:, h * DV:(h + 1) * DV],
                             lhsT=k3t_sb[:, h * 128:(h + 1) * 128],
                             rhs=vtj_all[j][:, h * DV:(h + 1) * DV],
                             start=True, stop=True, skip_group_check=True)
                    tU = k.inc("t")
                    k.wait("v", tU)
                    for h in range(NH):
                        if j == 0:
                            k.op("v", "tensor_copy", out=sl_f[:, h * DV:(h + 1) * DV],
                                 in_=ps_u[:, h * DV:(h + 1) * DV])
                        else:
                            k.op("v", "scalar_tensor_tensor", out=sl_f[:, h * DV:(h + 1) * DV],
                                 in0=sl_f[:, h * DV:(h + 1) * DV], scalar=ecols[:, 4 + h:5 + h],
                                 in1=ps_u[:, h * DV:(h + 1) * DV], op0=ALU.mult, op1=ALU.add)
                    k.op("v", "tensor_copy", out=sl_b[:, :], in_=sl_f[:, :])
                    prev_slb = k.inc("v")
            tOlast = tOj
            k.wait("s", tv_rt)
            for h in range(NH):
                k.op("s", "activation", out=fh23[:, h, 0:CH], in_=Cl3[:, h, 0:CH],
                     func=AF.Exp, scale=-1.0, bias=Cl3[:, h, CH:CH + 1])
            ts_f2 = k.inc("s")
            k.wait("v", ts_f2)
            k.op("v", "tensor_tensor", out=k2_sb[:, :], in0=k_sb[:, :], in1=fh2_sb[:, :],
                 op=ALU.mult)
            tvk2 = k.inc("v")
            k.wait("t", tvk2, ts_tp)
            for h in range(NH):
                k.op("t", "transpose", out=ps_tp[0:128, h * 128:(h + 1) * 128],
                     in_=k2_sb[:, h * CH:(h + 1) * CH], identity=C.ident_b[:])
            ttr2 = k.inc("t")
            k.wait("s", ttr2)
            k.op("s", "copy", out=k2t_sb[:, :], in_=ps_tp[0:128, 0:KD])
            ts_k2t = k.inc("s")
            k.wait("t", ts_k2t, prev_slb)
            for h in range(NH):
                k.op("t", "matmul", out=ps_u[:, h * DV:(h + 1) * DV],
                     lhsT=k2t_sb[:, h * 128:(h + 1) * 128],
                     rhs=vt_sb[:, h * DV:(h + 1) * DV],
                     start=True, stop=True, skip_group_check=True)
            tU2 = k.inc("t")
            k.wait("v", tU2)
            for h in range(NH):
                k.op("v", "scalar_tensor_tensor", out=S_f[:, h * DV:(h + 1) * DV],
                     in0=S_f[:, h * DV:(h + 1) * DV],
                     scalar=fh2_sb[:, h * CH:h * CH + 1],
                     in1=ps_u[:, h * DV:(h + 1) * DV], op0=ALU.mult, op1=ALU.add)
            k.op("v", "tensor_copy", out=S_b[:, :], in_=S_f[:, :])
            tSb = k.inc("v")
            prev_tC = ts_f2
            k.wait("s", tOlast)
            k.op("s", "copy", out=o_sb[:, 0:512], in_=ps_Oa[:, 0:512])
            k.op("s", "copy", out=o_sb[:, 512:1024], in_=ps_Ob[:, 0:512])
            k.op("s", "activation", out=o2_sb[:, 0:512], in_=ps_Oa[:, 0:512], func=AF.Square)
            k.op("s", "activation", out=o2_sb[:, 512:1024], in_=ps_Ob[:, 0:512], func=AF.Square)
            ts_o = k.inc("s")
            k.wait("t", ts_o, tSb)
            for hh in range(NH):
                for q in range(2):
                    k.op("t", "matmul", out=ps_pr[0:1, hh * CH:(hh + 1) * CH],
                         lhsT=C.ones_col_b[:],
                         rhs=o_sb[:, hh * DV + q * 128: hh * DV + (q + 1) * 128],
                         start=(q == 0), stop=(q == 1), skip_group_check=True)
                for q in range(2):
                    k.op("t", "matmul", out=ps_u[0:1, hh * CH:(hh + 1) * CH],
                         lhsT=C.ones_col_b[:],
                         rhs=o2_sb[:, hh * DV + q * 128: hh * DV + (q + 1) * 128],
                         start=(q == 0), stop=(q == 1), skip_group_check=True)
            tstat = k.inc("t")
            k.wait("v", tstat)
            k.op("v", "tensor_single_scalar", out=gnrow[0:1, 0:512],
                 in_=ps_pr[0:1, 0:512], scalar=1.0 / DV, op=ALU.mult)
            k.op("v", "tensor_single_scalar", out=gnrow[0:1, 512:1024],
                 in_=ps_u[0:1, 0:512], scalar=1.0 / DV, op=ALU.mult)
            k.op("v", "tensor_tensor", out=gnrow[0:1, 1024:1536],
                 in0=gnrow[0:1, 0:512], in1=gnrow[0:1, 0:512], op=ALU.mult)
            k.op("v", "tensor_tensor", out=gnrow[0:1, 1024:1536],
                 in0=gnrow[0:1, 512:1024], in1=gnrow[0:1, 1024:1536], op=ALU.subtract)
            tv_var = k.inc("v")
            k.wait("s", tv_var)
            k.op("s", "activation", out=gnrow[0:1, 1536:2048], in_=gnrow[0:1, 1024:1536],
                 func=AF.Ln, bias=C.eps_ln[0:1, 0:1], scale=1.0)
            k.op("s", "activation", out=gnrow[0:1, 1536:2048], in_=gnrow[0:1, 1536:2048],
                 func=AF.Exp, scale=-0.5)
            ts_gn = k.inc("s")
            k.wait("v", ts_gn)
            k.op("v", "tensor_tensor", out=gnrow[0:1, 1024:1536],
                 in0=gnrow[0:1, 0:512], in1=gnrow[0:1, 1536:2048], op=ALU.mult)
            k.op("v", "tensor_single_scalar", out=gnrow[0:1, 1024:1536],
                 in_=gnrow[0:1, 1024:1536], scalar=-1.0, op=ALU.mult)
            tv_nb = k.inc("v")
            k.wait("t", tv_nb)
            k.op("t", "matmul", out=ps_big[:, 0:512], lhsT=C.ones_row_f[:],
                 rhs=gnrow[0:1, 1536:2048], start=True, stop=True, skip_group_check=True)
            k.op("t", "matmul", out=ps_big[:, 512:1024], lhsT=C.ones_row_f[:],
                 rhs=gnrow[0:1, 1024:1536], start=True, stop=True, skip_group_check=True)
            tbc = k.inc("t")
            k.wait("v", tbc, tsg)
            for h in range(NH):
                for q in range(2):
                    dcg = h * 2 + q
                    k.op("v", "tensor_tensor", out=on_sb[:, dcg * 128:(dcg + 1) * 128],
                         in0=o_sb[:, h * DV + q * 128: h * DV + (q + 1) * 128],
                         in1=ps_big[:, h * 128:(h + 1) * 128], op=ALU.mult)
            for h in range(NH):
                for q in range(2):
                    dcg = h * 2 + q
                    k.op("v", "tensor_tensor", out=on_sb[:, dcg * 128:(dcg + 1) * 128],
                         in0=on_sb[:, dcg * 128:(dcg + 1) * 128],
                         in1=ps_big[:, 512 + h * 128: 512 + (h + 1) * 128], op=ALU.add)
            for dcg in range(DC):
                k.op("v", "tensor_scalar", out=on_sb[:, dcg * 128:(dcg + 1) * 128],
                     in0=on_sb[:, dcg * 128:(dcg + 1) * 128],
                     scalar1=gnw_c[:, dcg:dcg + 1], scalar2=gnb_c[:, dcg:dcg + 1],
                     op0=ALU.mult, op1=ALU.add)
            for dcg in range(DC):
                k.op("v", "tensor_tensor", out=on_sb[:, dcg * 128:(dcg + 1) * 128],
                     in0=on_sb[:, dcg * 128:(dcg + 1) * 128],
                     in1=g_sb[:, dcg * 128:(dcg + 1) * 128], op=ALU.mult)
            ton = k.inc("v")

            k.wait("t", ton)
            for half in range(2):
                for dc in range(DC):
                    k.op("t", "matmul", out=ps_big[:, half * 512:(half + 1) * 512],
                         lhsT=on_sb[:, dc * 128:(dc + 1) * 128],
                         rhs=ow[:, dc * D + half * 512: dc * D + (half + 1) * 512],
                         start=(dc == 0), stop=(dc == DC - 1))
            tsp = k.inc("t")
            ddx = k.dma("q", out=xt_f[:, :], in_=x1_r[:, ci, :], wait=(k.tok("v"),))
            ddy = k.dma("q", out=ymt[:, :], in_=ym_r[:, ci, :])
            k.wait("s", ddy)
            k.op("s", "activation", out=g1t[:, :], in_=ymt[:, :], func=AF.Silu)
            ts9 = k.inc("s")
            k.wait("v", tsp, ts9, ddx)
            k.op("v", "tensor_tensor", out=g1t[:, :], in0=ps_big[:, 0:D], in1=g1t[:, :], op=ALU.mult)
            tv = k.inc("v")
            k.wait("v", tv)
            k.op("v", "tensor_tensor", out=xt_f[:, :], in0=xt_f[:, :], in1=g1t[:, :], op=ALU.add)
            tv = k.inc("v")
            k.dma("q", out=y_r[:, ci, :], in_=xt_f[:, :], wait=(tv,))
            k.wait("v", ("d", "q", k.dcnt["q"]))
            k.wait("t", k.tok("v"))
        k.fence_all()


def _f32c(a):
    return np.ascontiguousarray(np.asarray(a, dtype=np.float32))


L1_PHASES = "lgms"
MOE_FENCE_H = False
MOE_FENCE_Y = False
MOE_FENCE_TSB = False
MOE_FENCE_E = False
MOE_EXPERTS = NE


def emit_l1(k, W, x1_in, ymoe_scr, y_out, dbg=None):
    nc, es = k.nc, k.es
    S = lambda *a, **kw: es.enter_context(nc.sbuf_tensor(*a, **kw))
    xn1_fm = S("l1_xn_fm", [128, DC * TS], BF16)
    dw_sb = S("l1_dw", [128, NT * NE], F32)
    l1es = contextlib.ExitStack()
    xn_lo_fm = l1es.enter_context(nc.sbuf_tensor("l1_xn_lo", [128, DC * T], BF16))
    if "l" in L1_PHASES:
        emit_l1_ln(k, W, x1_in, xn1_fm, xn_lo_fm, dbg=dbg)
    if dbg and "xn1_fm" in dbg:
        k.dma("q", out=dbg["xn1_fm"], in_=xn1_fm.ap())
        k.fence_all()
    if "g" in L1_PHASES:
        emit_l1_gate(k, W, xn1_fm, xn_lo_fm, dw_sb, dbg=dbg)
    l1es.close()
    if dbg and "dw" in dbg:
        k.dma("q", out=dbg["dw"], in_=dw_sb.ap())
        k.fence_all()
    if "m" in L1_PHASES:
        emit_l1_moe(k, W, xn1_fm, dw_sb, ymoe_scr, dbg=dbg)
    if "s" in L1_PHASES:
        emit_l1_scan(k, W, x1_in, ymoe_scr, y_out, xn1_fm, dbg=dbg)
    else:
        for i in range(NT):
            k.dma("g", out=y_out.rearrange("(n p) d -> p n d", p=128)[:, i, :],
                  in_=xn1_fm[:, 0:D])
        k.fence_all()


WEIGHT_SPECS_L1 = {
    "gate_wT": [D, NE],
    "moe_w1T": [NE, D, NI],
    "moe_w2T": [NE, NI, D],
    "moe_w3T": [NE, D, NI],
    "x_proj_mu_c": [128, DC],
    "xl_w1T": [D, RLO],
    "xl_w2T": [32, R5],
    "xlb2_cols": [128, 2],
    "x_out_wTn": [33, 5 * D],
    "r_wT": [D, KD],
    "k_wT": [D, KD],
    "v_wT": [D, VD],
    "g_wT": [D, VD],
    "o_wT": [VD, D],
    "w_w1T": [D, GD],
    "w_w2T": [GD, KD],
    "w_b_c": [128, NH],
    "bonus_c": [128, NH],
    "gn_w_c": [128, DC],
    "gn_b_c": [128, DC],
}


def host_prep_l1(inputs):
    o = {}
    g = lambda nm: np.asarray(inputs[nm], dtype=np.float32)
    o["gate_wT"] = _f32c(g("L1_ffn_gate_w").T)
    o["moe_w1T"] = _f32c(g("L1_ffn_w1").transpose(0, 2, 1))
    o["moe_w2T"] = _f32c(g("L1_ffn_w2").transpose(0, 2, 1))
    o["moe_w3T"] = _f32c(g("L1_ffn_w3").transpose(0, 2, 1))
    o["x_proj_mu_c"] = _f32c(g("L1_attn_x_proj_mu").reshape(DC, 128).T)
    o["xl_w1T"] = _f32c(g("L1_attn_xl_w1").T)
    o["xl_w2T"] = _f32c(g("L1_attn_xl_w2").T)
    b2 = g("L1_attn_xl_b2")
    xc = np.zeros((128, 2), np.float32)
    xc[0:96, 0] = b2[0:96]
    xc[0:64, 1] = b2[96:160]
    o["xlb2_cols"] = xc
    xo = g("L1_attn_x_out_w").reshape(D, 5, 32)
    xow = np.concatenate([xo[:, n, :].T for n in range(5)], axis=1)
    xb = g("L1_attn_x_bias").reshape(1, 5 * D)
    o["x_out_wTn"] = _f32c(np.concatenate([xow, xb], axis=0))
    o["r_wT"] = _f32c(g("L1_attn_r_w").T)
    o["k_wT"] = _f32c(g("L1_attn_k_w").T)
    o["v_wT"] = _f32c(g("L1_attn_v_w").T)
    o["g_wT"] = _f32c(g("L1_attn_g_w").T)
    o["o_wT"] = _f32c(g("L1_attn_o_w").T)
    o["w_w1T"] = _f32c(g("L1_attn_w_w1").T)
    o["w_w2T"] = _f32c(g("L1_attn_w_w2").T)
    o["w_b_c"] = _f32c(g("L1_attn_w_b").reshape(NH, 128).T)
    o["bonus_c"] = _f32c(g("L1_attn_bonus").T)
    o["gn_w_c"] = _f32c(g("L1_attn_gn_weight").reshape(DC, 128).T)
    o["gn_b_c"] = _f32c(g("L1_attn_gn_bias").reshape(DC, 128).T)
    return o


def build_l1_program(dbg_names=()):
    nc = bass.Bass("TRN2", target_bir_lowering=False)
    x1 = nc.dram_tensor("x1_in", [T, D], F32, kind="ExternalInput").ap()
    y = nc.dram_tensor("y", [T, D], F32, kind="ExternalOutput").ap()
    W = {nm: nc.dram_tensor(nm, shape, F32, kind="ExternalInput").ap()
         for nm, shape in WEIGHT_SPECS_L1.items()}
    ymoe = nc.dram_tensor("ymoe", [T, D], F32,
                          kind="ExternalOutput" if "ymoe" in dbg_names else "Internal").ap()
    dbg = {}
    specs = {"xn1_fm": ([128, DC * TS], BF16), "dw": ([128, NT * NE], F32)}
    for nm in set(dbg_names) & set(specs):
        shape, dt = specs[nm]
        dbg[nm] = nc.dram_tensor("dbg_" + nm, shape, dt, kind="ExternalOutput").ap()
    if "ymoe" in dbg_names:
        dbg["ymoe"] = True
    es = contextlib.ExitStack()
    with es:
        k = K(nc, es)
        block = es.enter_context(nc.Block())

        @block.sync
        def _(sync):
            alloc_consts(k)
            emit_l1(k, W, x1, ymoe, y, dbg=dbg if dbg else None)
            k.fence_all()
    return nc


def build_full_program():
    nc = bass.Bass("TRN2", target_bir_lowering=False)
    x_in = nc.dram_tensor("x_in", [T, D], F32, kind="ExternalInput").ap()
    y = nc.dram_tensor("y", [T, D], F32, kind="ExternalOutput").ap()
    W = {nm: nc.dram_tensor(nm, shape, F32, kind="ExternalInput").ap()
         for nm, shape in WEIGHT_SPECS_L0.items()}
    for nm, shape in WEIGHT_SPECS_L1.items():
        W[nm] = nc.dram_tensor(nm, shape, F32, kind="ExternalInput").ap()
    W["w1t_scr"] = nc.dram_tensor("w1t_scr", [D, FF], BF16, kind="Internal").ap()
    W["w2t_scr"] = nc.dram_tensor("w2t_scr", [FF, D], BF16, kind="Internal").ap()
    x1 = nc.dram_tensor("x1_scr", [T, D], F32, kind="Internal").ap()
    ymoe = nc.dram_tensor("ymoe_scr", [T, D], F32, kind="Internal").ap()
    es = contextlib.ExitStack()
    with es:
        k = K(nc, es)
        block = es.enter_context(nc.Block())

        @block.sync
        def _(sync):
            alloc_consts(k)
            emit_l0(k, W, x_in, x1, dbg=None)
            emit_l1(k, W, x1, ymoe, y, dbg=None)
            k.fence_all()
    return nc



def host_prep_l0(inputs):
    o = {}
    o["w_qkvT"] = _f32c(np.asarray(inputs["L0_attn_w_qkv"]).T)
    b = np.asarray(inputs["L0_attn_b_qkv"], dtype=np.float32)
    o["bqkv_qk_c"] = _f32c(b[:2048].reshape(16, 128).T)
    o["bqkv_v_row"] = _f32c(b[2048:].reshape(1, D))
    o["w_outT"] = _f32c(np.asarray(inputs["L0_attn_w_out"]).T)
    o["b_out_row"] = _f32c(np.asarray(inputs["L0_attn_b_out"]).reshape(1, D))
    o["w1T"] = _f32c(np.asarray(inputs["L0_ffn_W1"]).T)
    o["b1_c"] = _f32c(np.asarray(inputs["L0_ffn_b1"]).reshape(32, 128).T)
    o["w2T"] = _f32c(np.asarray(inputs["L0_ffn_W2"]).T)
    o["b2_row"] = _f32c(np.asarray(inputs["L0_ffn_b2"]).reshape(1, D))
    return o


WEIGHT_SPECS_L0 = {
    "w_qkvT": [D, 3 * D],
    "bqkv_qk_c": [128, 16],
    "bqkv_v_row": [1, D],
    "w_outT": [D, D],
    "b_out_row": [1, D],
    "w1T": [D, FF],
    "b1_c": [128, 32],
    "w2T": [FF, D],
    "b2_row": [1, D],
}


def build_l0_program(dbg_names=()):
    nc = bass.Bass("TRN2", target_bir_lowering=False)
    x_in = nc.dram_tensor("x_in", [T, D], F32, kind="ExternalInput").ap()
    x1 = nc.dram_tensor("x1", [T, D], F32, kind="ExternalOutput").ap()
    W = {nm: nc.dram_tensor(nm, shape, F32, kind="ExternalInput").ap()
         for nm, shape in WEIGHT_SPECS_L0.items()}
    W["w1t_scr"] = nc.dram_tensor("w1t_scr", [D, FF], BF16, kind="Internal").ap()
    W["w2t_scr"] = nc.dram_tensor("w2t_scr", [FF, D], BF16, kind="Internal").ap()
    dbg = {}
    specs = {"xn0_fm": ([128, DC * T], BF16), "q_ext": ([128, DC * (T + 16)], BF16),
             "k_fm": ([128, DC * T], BF16), "v_tm": ([128, NT * H0 * (HD0 + 1)], BF16),
             "s_tm": ([128, NT * D], BF16)}
    want = set(dbg_names)
    if "qkv" in want:
        want |= {"q_ext", "k_fm", "v_tm"}
    for nm in want & set(specs):
        shape, dt = specs[nm]
        dbg[nm] = nc.dram_tensor("dbg_" + nm, shape, dt, kind="ExternalOutput").ap()
    if "qkv" in dbg_names:
        dbg["qkv"] = True
    es = contextlib.ExitStack()
    with es:
        k = K(nc, es)
        block = es.enter_context(nc.Block())

        @block.sync
        def _(sync):
            alloc_consts(k)
            emit_l0(k, W, x_in, x1, dbg=dbg if dbg else None)
            k.fence_all()
    return nc


_PROG_CACHE = {}


def kernel(**inputs):
    x_full = np.ascontiguousarray(np.asarray(inputs["L0_attn_input_embeddings"], dtype=np.float32))
    W = host_prep_l0(inputs)
    W.update(host_prep_l1(inputs))
    if "full" not in _PROG_CACHE:
        _PROG_CACHE["full"] = build_full_program()
    nc = _PROG_CACHE["full"]
    in_maps = []
    for core in range(8):
        m = dict(W)
        m["x_in"] = np.ascontiguousarray(x_full[core])
        in_maps.append(m)
    res = run_bass_kernel_spmd(nc, in_maps, core_ids=list(range(8)))
    out = np.stack([np.asarray(res.results[i]["y"], dtype=np.float32) for i in range(8)], axis=0)
    return out
